# version 01
# baseline (speedup 1.0000x reference)
import numpy as np
import ml_dtypes
from contextlib import ExitStack

import concourse.bass as bass
import concourse.mybir as mybir
from concourse.bass_utils import run_bass_kernel_spmd

F32 = mybir.dt.float32
BF16 = mybir.dt.bfloat16
U32 = mybir.dt.uint32
AX = mybir.AxisListType.X
OP = mybir.AluOpType
AF = mybir.ActivationFunctionType

NCORES = 8
D = 1024
H = 16
HD = 64
B = 2
S = 2048
T = B * S
NF = D // 128
E = 8
CAP = 100
JW = 1025
JPAD = 1152
EPS = 1e-5
CH = 512
NCH = T // CH
DEBUG = False


class MK:
    def __init__(self, nc, st, stop_after="full"):
        self.nc = nc
        self.st = st
        self.stop_after = stop_after
        self.sem = {}
        self.cnt = {}
        self.mk_sems("qS", "qA", "qG", "qL0", "qL1", "qSt", "T", "V", "S", "G", "C")
        self.mk_dram()
        self.mk_sbuf()
        self.mk_psum()

    def mk_sems(self, *names):
        for n in names:
            self.sem[n] = self.st.enter_context(self.nc.semaphore(n))
            self.cnt[n] = 0

    def inc(self, instr, name, n=1):
        instr.then_inc(self.sem[name], n)
        self.cnt[name] += n
        return self.cnt[name]

    def wait(self, eng, name, v=None):
        eng.wait_ge(self.sem[name], self.cnt[name] if v is None else v)

    def mk_dram(self):
        nc = self.nc

        def din(name, shape, dtype=F32):
            setattr(self, name,
                    nc.dram_tensor(name, shape, dtype, kind="ExternalInput").ap())
        din("xin", [D, T])
        din("wqT", [D, D]); din("wkT", [D, D]); din("wvT", [D, D]); din("woT", [D, D])
        din("wed", [D, D]); din("wgd", [D, E])
        din("pekTd", [128, JPAD])
        din("pevd", [JPAD, HD])
        din("bexpd", [1, D])
        din("idend", [128, 128])
        din("antid", [128, 128])
        din("iota8d", [E, 1])
        din("selmatd", [E, E])
        self.out = nc.dram_tensor("out", [D, T], F32, kind="ExternalOutput").ap()
        if DEBUG:
            self.dbg = nc.dram_tensor("dbg", [128, 16896], F32,
                                      kind="ExternalOutput").ap()
        self.xd = nc.dram_tensor("xd", [D, T], F32, kind="Internal").ap()
        self.cc_in = [nc.dram_tensor(f"ccin{i}", [128, 128], F32,
                                     kind="Internal").ap() for i in range(2)]
        self.cc_out = [nc.dram_tensor(f"ccout{i}", [128, 128], F32,
                                      kind="Internal").ap() for i in range(2)]

    def mk_sbuf(self):
        def sb(name, shape, dtype):
            setattr(self, name, self.st.enter_context(
                self.nc.sbuf_tensor(name, shape, dtype)))
        sb("idenf", [128, 128], F32)
        sb("idenb", [128, 128], BF16)
        sb("onescol", [128, 1], F32)
        sb("onesb", [128, 1], BF16)
        sb("onesrow", [1, 128], F32)
        sb("onesP", [65, 128], F32)
        sb("wgs", [128, NF * E], F32)
        sb("wgcs", [E, 1], F32)
        sb("bexps", [1, D], BF16)
        sb("pekTs", [128, JPAD], BF16)
        sb("pevs", [128, (JPAD // 128) * HD], BF16)
        sb("onesbr", [1, 128], BF16)
        sb("antib", [128, 128], BF16)
        sb("pev0r", [1, HD], BF16)
        sb("pev0bc", [128, HD], BF16)
        sb("iotac", [E, 1], F32)
        sb("selm", [E, E], F32)
        sb("colmeta", [128, 8 * (T // 128)], F32)
        sb("rch", [1, 6 * CH], F32)
        sb("geffb_ch", [1, CH], BF16)
        sb("zerE", [E, CH], F32)
        sb("cnt8", [E, 1], F32)
        sb("off8", [E, 1], F32)
        sb("ccs", [128, 128], F32)
        sb("cct", [E, E], F32)
        sb("tm_lg", [128, E], F32)
        sb("tm_mx8", [128, E], F32)
        sb("tm_exp", [128, E], F32)
        sb("tm_idx", [128, 8], U32)
        sb("smeta0", [128, 8], F32)
        sb("smeta1", [128, 8], F32)
        sb("ameta", [128, 4], F32)
        sb("bmeta", [128, 2], F32)
        sb("bn6", [128, 12], F32)
        sb("mvar", [128, 2], F32)
        self.smeta = [self.smeta0, self.smeta1]
        self.CM = {"r": 0, "mr": 1, "gate": 2, "keep": 3, "geff": 4,
                   "ga": 5, "gb": 6}
        self.RCH = {"r": 0, "mr": 1, "ga": 2, "gb": 3, "best": 4, "keep": 5}
        sb("arenaF", [128, 28672], F32)
        sb("arenaB", [128, 32768], BF16)
        aF = self.arenaF
        self.aFB = aF[:].bitcast(BF16)
        self.xw0 = aF[:, 0:4096]
        self.xw1 = aF[:, 4096:8192]
        self.lnt = aF[:, 8192:12288]
        self.lgT = aF[0:E, 12288:12288 + T]
        self.mT = aF[0:E, 16384:16384 + T]
        self.lnw = self.arenaB[:, 0:4096]
        self.wes = self.arenaB[:, 4096:12288]
        self.xw = [self.xw0, self.xw1]

    def col(self, name, tt):
        return self.colmeta[:, 8 * tt + self.CM[name]:8 * tt + self.CM[name] + 1]

    def colslice(self, name):
        import concourse.bass as _b
        NT = T // 128
        return _b.AP(self.colmeta, self.CM[name], [[8 * NT, 128], [8, NT]])

    def rowch(self, name, w=CH):
        o = CH * self.RCH[name]
        return self.rch[0:1, o:o + w]

    def mk_psum(self):
        def ps(name, shape, dtype=F32):
            setattr(self, name, self.st.enter_context(
                self.nc.psum_tensor(name, shape, dtype)))
        ps("pq20", [128, 1024])
        ps("pq21", [128, 1024])
        ps("pp0", [128, 512])
        ps("pp1", [128, 512])
        ps("pmini", [128, 512])
        ps("pmini2", [128, 512])
        self.pq2 = [self.pq20, self.pq21]
        self.pp = [self.pp0, self.pp1]

    def p_init(self, block):
        mk = self

        @block.sync
        def _(sync):
            mk.inc(sync.dma_start(mk.idenf[:], mk.idend[:]), "qS", 16)
            for f in range(NF):
                mk.inc(sync.dma_start(mk.wgs[:, E * f:E * f + E],
                                      mk.wgd[128 * f:128 * f + 128, :]),
                       "qS", 16)
            mk.inc(sync.dma_start(mk.iotac[:], mk.iota8d[:]), "qS", 16)
            mk.inc(sync.dma_start(mk.selm[:], mk.selmatd[:]), "qS", 16)

        @block.gpsimd
        def _(g):
            mk.inc(g.dma_start(mk.pekTs[:], mk.pekTd[:]), "qG", 16)
            for c in range(JPAD // 128):
                mk.inc(g.dma_start(mk.pevs[:, HD * c:HD * c + HD],
                                   mk.pevd[128 * c:128 * c + 128, :]),
                       "qG", 16)
            mk.inc(g.dma_start(mk.bexps[:], mk.bexpd[:]), "qG", 16)
            mk.inc(g.dma_start(mk.pev0r[:], mk.pevd[0:1, :]), "qG", 16)
            mk.inc(g.dma_start(mk.antib[:], mk.antid[:]), "qG", 16)

        @block.vector
        def _(vector):
            mk.wait(vector, "qS", 176)
            vector.tensor_copy(mk.idenb[:], mk.idenf[:])
            vector.memset(mk.onescol[:], 1.0)
            vector.memset(mk.onesb[:], 1.0)
            vector.memset(mk.onesrow[:], 1.0)
            vector.memset(mk.onesbr[:], 1.0)
            vector.memset(mk.onesP[:], 1.0)
            vector.memset(mk.zerE[:], 0.0)
            mk.inc(vector.memset(mk.ccs[:], 0.0), "V")
        self.init_V = self.cnt["V"]

        @block.tensor
        def _(tensor):
            mk.wait(tensor, "qS", 176)
            mk.wait(tensor, "V", self.init_V)
            last = None
            for k in range(NF):
                last = tensor.matmul(
                    mk.pmini[0:E, 0:1], mk.wgs[:, E * k:E * k + E],
                    mk.onescol[:, 0:1], start=(k == 0), stop=(k == NF - 1))
            mk.inc(last, "T")

        @block.vector
        def _(vector):
            mk.wait(vector, "T")
            mk.inc(vector.tensor_copy(mk.wgcs[:], mk.pmini[0:E, 0:1]), "V")

        @block.tensor
        def _(tensor):
            mk.wait(tensor, "qG")
            mk.wait(tensor, "V", mk.cnt["V"])
            mk.inc(tensor.matmul(mk.pmini[:, 128:128 + HD],
                                 mk.onesbr[0:1, :], mk.pev0r[0:1, :],
                                 start=True, stop=True), "T")

        @block.vector
        def _(vector):
            mk.wait(vector, "T")
            mk.wait(vector, "qG")
            v = mk.inc(vector.tensor_copy(mk.pev0bc[:],
                                          mk.pmini[:, 128:128 + HD]), "V")
            vector.wait_ge(mk.sem["V"], v)
            last = None
            for c in range(JPAD // 128):
                last = vector.tensor_tensor(
                    mk.pevs[:, HD * c:HD * c + HD],
                    mk.pevs[:, HD * c:HD * c + HD], mk.pev0bc[:],
                    OP.subtract)
            mk.inc(last, "V")

    def p_ffn(self, block, idx, src, dst, pre_wait=()):
        mk = self

        def load_chunk(sync, i, buf):
            q = f"qL{i % 2}"
            if mk.cnt[q] > 0:
                mk.wait(sync, q)
            last = None
            for f in range(NF):
                last = mk.inc(sync.dma_start(
                    buf[:, CH * f:CH * f + CH],
                    src[128 * f:128 * f + 128, CH * i:CH * i + CH]), q, 16)
            return (q, last)

        def store_chunk(sync, i, buf):
            if mk.cnt["qSt"] > 0:
                mk.wait(sync, "qSt")
            last = None
            for f in range(NF):
                last = mk.inc(sync.dma_start(
                    dst[128 * f:128 * f + 128, CH * i:CH * i + CH],
                    buf[:, CH * f:CH * f + CH]), "qSt", 16)
            return ("qSt", last)

        ld = {}
        bn_V = {}
        recip_V = {}
        mrT = {}
        gate_T = {}
        fix_V = {}

        @block.sync
        def _(sync):
            for s, v in pre_wait:
                mk.wait(sync, s, v)
            ld[0] = load_chunk(sync, 0, mk.xw[0])
            ld[1] = load_chunk(sync, 1, mk.xw[1])

        @block.gpsimd
        def _(g):
            for s, v in pre_wait:
                mk.wait(g, s, v)
            if mk.cnt["qG"] > 0:
                mk.wait(g, "qG")
            for k in range(NF):
                mk.inc(g.dma_start(mk.wes[:, D * k:D * k + D],
                                   mk.wed[128 * k:128 * k + 128, :]),
                       "qG", 16)
        wes_qG = mk.cnt["qG"]

        for i in range(NCH):
            b = i % 2
            for tt in range(4):
                k = 4 * i + tt

                @block.tensor
                def _(tensor, i=i, b=b, tt=tt, k=k):
                    mk.wait(tensor, *ld[i])
                    if k - 2 >= 0:
                        mk.wait(tensor, "V", bn_V[k - 2])
                    if k <= 1:
                        for s, v in pre_wait:
                            mk.wait(tensor, s, v)
                    last = None
                    for f in range(NF):
                        last = tensor.transpose(
                            mk.pq2[k % 2][:, 128 * f:128 * f + 128],
                            mk.xw[b][:, CH * f + 128 * tt:
                                     CH * f + 128 * tt + 128],
                            mk.idenf[:])
                    mk.inc(last, "T")
                tile_T = mk.cnt["T"]

                @block.vector
                def _(vector, k=k, tt=tt, i=i, tile_T=tile_T):
                    mk.wait(vector, "T", tile_T)
                    p = mk.pq2[k % 2]
                    sm = mk.smeta[k % 2]
                    vector.bn_stats(mk.bn6[:, 0:6], p[:, 0:512])
                    v = mk.inc(vector.bn_stats(mk.bn6[:, 6:12],
                                               p[:, 512:1024]), "V")
                    bn_V[k] = v
                    vector.wait_ge(mk.sem["V"], v)
                    v = mk.inc(vector.bn_aggr(mk.mvar[:], mk.bn6[:, 0:12]),
                               "V")
                    vector.wait_ge(mk.sem["V"], v)
                    if k - 2 >= 0:
                        mk.wait(vector, "T", mrT[k - 2])
                    vector.tensor_copy(sm[:, 0:1], mk.mvar[:, 0:1])
                    mk.inc(vector.tensor_scalar(
                        sm[:, 5:6], mk.mvar[:, 1:2], EPS, None, OP.add), "V")
                sq_in = mk.cnt["V"]

                @block.scalar
                def _(scalar, k=k, sq_in=sq_in):
                    mk.wait(scalar, "V", sq_in)
                    mk.inc(scalar.activation(
                        mk.smeta[k % 2][:, 6:7], mk.smeta[k % 2][:, 5:6],
                        AF.Sqrt), "S")
                sq_out = mk.cnt["S"]

                @block.vector
                def _(vector, k=k, tt=tt, i=i, sq_out=sq_out):
                    mk.wait(vector, "S", sq_out)
                    sm = mk.smeta[k % 2]
                    v = mk.inc(vector.reciprocal(sm[:, 1:2], sm[:, 6:7]), "V")
                    vector.wait_ge(mk.sem["V"], v)
                    v = mk.inc(vector.tensor_mul(sm[:, 2:3], sm[:, 0:1],
                                                 sm[:, 1:2]), "V")
                    vector.wait_ge(mk.sem["V"], v)
                    recip_V[k] = mk.inc(vector.tensor_copy(
                        mk.colmeta[:, 8 * k:8 * k + 2], sm[:, 1:3]), "V")

                @block.tensor
                def _(tensor, i=i, tt=tt, k=k):
                    mk.wait(tensor, "V", recip_V[k])
                    if tt == 0 and i > 0:
                        mk.wait(tensor, "V", fix_V[i - 1])
                    tensor.transpose(
                        mk.pmini[0:1, 128 * tt:128 * tt + 128],
                        mk.smeta[k % 2][:, 1:2], mk.idenf[:])
                    mrT[k] = mk.inc(tensor.transpose(
                        mk.pmini2[0:1, 128 * tt:128 * tt + 128],
                        mk.smeta[k % 2][:, 2:3], mk.idenf[:]), "T")

            @block.tensor
            def _(tensor, i=i, b=b):
                if i >= 1:
                    mk.wait(tensor, "V", fix_V[i - 1])
                last = None
                for k in range(NF):
                    last = tensor.matmul(
                        mk.pp[0][0:E, 0:CH],
                        mk.wgs[:, E * k:E * k + E],
                        mk.xw[b][:, CH * k:CH * k + CH],
                        start=(k == 0), stop=(k == NF - 1))
                gate_T[i] = mk.inc(last, "T")
            xw_freeA = ("T", gate_T[i])

            @block.vector
            def _(vector, i=i):
                mk.wait(vector, "T", mrT[4 * i + 3])
                v = mk.inc(vector.tensor_copy(mk.rowch("r"),
                                              mk.pmini[0:1, 0:CH]), "V")
                mk.inc(vector.tensor_copy(mk.rowch("mr"),
                                          mk.pmini2[0:1, 0:CH]), "V")
            mr_V = mk.cnt["V"]

            @block.tensor
            def _(tensor, i=i, mr_V=mr_V):
                mk.wait(tensor, "V", mr_V)
                tensor.matmul(mk.pp[1][0:E, 0:CH], mk.onesrow[0:1, 0:E],
                              mk.rowch("mr"), start=True, stop=True)
                mk.inc(tensor.matmul(
                    mk.pmini[0:E, 0:CH], mk.onesrow[0:1, 0:E],
                    mk.rowch("r"), start=True, stop=True), "T")
            bc_T = mk.cnt["T"]

            @block.vector
            def _(vector, i=i, bc_T=bc_T):
                mk.wait(vector, "T", bc_T)
                c0 = CH * i
                lg = mk.lgT[:, c0:c0 + CH]
                v = mk.inc(vector.tensor_copy(lg, mk.pp[0][0:E, 0:CH]), "V")
                vector.wait_ge(mk.sem["V"], v)
                v = mk.inc(vector.scalar_tensor_tensor(
                    lg, mk.pp[1][0:E, 0:CH], mk.wgcs[0:E, 0:1], lg,
                    OP.mult, OP.subtract), "V")
                vector.wait_ge(mk.sem["V"], v)
                v = mk.inc(vector.tensor_mul(lg, lg, mk.pmini[0:E, 0:CH]),
                           "V")
                vector.wait_ge(mk.sem["V"], v)
                fix_V[i] = mk.inc(vector.tensor_scalar(
                    lg, lg, -1.0, None, OP.mult), "V")

            @block.sync
            def _(sync, i=i):
                if i + 2 < NCH:
                    s, v = xw_freeA
                    mk.wait(sync, s, v)
                    ld[i + 2] = load_chunk(sync, i + 2, mk.xw[i % 2])

        NT = T // 128
        am_T = {}
        am_psfree = {}
        exp_S_hist = {}
        bmT_hist = {}
        mt_V = {}
        for tt in range(NT):
            @block.tensor
            def _(tensor, tt=tt):
                if tt == 0:
                    mk.wait(tensor, "V", fix_V[NCH - 1])
                if tt - 2 >= 0:
                    s, v = am_psfree[tt - 2]
                    mk.wait(tensor, s, v)
                pdst = [mk.pmini, mk.pmini2][tt % 2]
                am_T[tt] = mk.inc(tensor.transpose(
                    pdst[0:128, 0:E], mk.lgT[:, 128 * tt:128 * tt + 128],
                    mk.idenf[0:E, 0:E]), "T")

            @block.vector
            def _(vector, tt=tt):
                mk.wait(vector, "T", am_T[tt])
                if tt > 0:
                    mk.wait(vector, "S", exp_S_hist[tt - 1])
                    mk.wait(vector, "T", bmT_hist[tt - 1])
                psrc = [mk.pmini, mk.pmini2][tt % 2]
                v = mk.inc(vector.tensor_copy(mk.tm_lg[:],
                                              psrc[0:128, 0:E]), "V")
                am_psfree[tt] = ("V", v)
                vector.wait_ge(mk.sem["V"], v)
                v = mk.inc(vector.reduce_max(mk.ameta[:, 0:1], mk.tm_lg[:],
                                             axis=AX), "V")
                vector.wait_ge(mk.sem["V"], v)
                vector.tensor_scalar(mk.tm_mx8[:], mk.tm_lg[:], 0.0,
                                     mk.ameta[:, 0:1], OP.mult, OP.add)
                v = mk.inc(vector.tensor_scalar(
                    mk.ameta[:, 1:2], mk.ameta[:, 0:1], -1.0, None, OP.mult),
                    "V")
                vector.wait_ge(mk.sem["V"], v)
                v = mk.inc(vector.max_index(mk.tm_idx[:], mk.tm_mx8[:],
                                            mk.tm_lg[:]), "V")
                vector.wait_ge(mk.sem["V"], v)
                mk.inc(vector.tensor_copy(mk.bmeta[:, 0:1],
                                          mk.tm_idx[:, 0:1]), "V")
            am_V = mk.cnt["V"]

            @block.scalar
            def _(scalar, tt=tt, am_V=am_V):
                mk.wait(scalar, "V", am_V)
                exp_S_hist[tt] = mk.inc(scalar.activation(
                    mk.tm_exp[:], mk.tm_lg[:], AF.Exp,
                    bias=mk.ameta[:, 1:2], scale=1.0,
                    accum_out=mk.ameta[:, 2:3]), "S")

            @block.vector
            def _(vector, tt=tt):
                mk.wait(vector, "S", exp_S_hist[tt])
                v = mk.inc(vector.reciprocal(mk.bmeta[:, 1:2],
                                             mk.ameta[:, 2:3]), "V")
                vector.wait_ge(mk.sem["V"], v)
                mk.inc(vector.tensor_copy(mk.col("gate", tt),
                                          mk.bmeta[:, 1:2]), "V")
            bm_V = mk.cnt["V"]

            @block.tensor
            def _(tensor, tt=tt, bm_V=bm_V):
                mk.wait(tensor, "V", bm_V)
                c = 128 * (tt % 4)
                bmT_hist[tt] = mk.inc(tensor.transpose(
                    mk.pq2[0][0:1, c:c + 128],
                    mk.bmeta[:, 0:1], mk.idenf[:]), "T")

            if tt % 4 == 3:
                i = tt // 4

                @block.vector
                def _(vector, i=i, tt=tt):
                    mk.wait(vector, "T", bmT_hist[tt])
                    mk.inc(vector.tensor_copy(mk.rowch("best"),
                                              mk.pq2[0][0:1, 0:CH]), "V")
                br_V = mk.cnt["V"]

                @block.tensor
                def _(tensor, i=i, br_V=br_V):
                    mk.wait(tensor, "V", br_V)
                    if i > 0:
                        mk.wait(tensor, "V", mt_V[i - 1])
                    mk.inc(tensor.matmul(
                        mk.pq2[1][0:E, 0:CH], mk.onesrow[0:1, 0:E],
                        mk.rowch("best"), start=True, stop=True), "T")
                bt_T = mk.cnt["T"]

                @block.vector
                def _(vector, i=i, bt_T=bt_T):
                    mk.wait(vector, "T", bt_T)
                    mt_V[i] = mk.inc(vector.tensor_scalar(
                        mk.mT[:, CH * i:CH * i + CH], mk.pq2[1][0:E, 0:CH],
                        mk.iotac[0:E, 0:1], None, OP.is_equal), "V")

        @block.vector
        def _(vector):
            mk.wait(vector, "V", mt_V[NCH - 1])
            for i in range(NCH):
                sl = slice(CH * i, CH * i + CH)
                init = 0.0 if i == 0 else mk.lgT[:, CH * i - 1:CH * i]
                v = mk.inc(vector.tensor_tensor_scan(
                    mk.lgT[:, sl], mk.mT[:, sl], mk.zerE[:], init,
                    OP.add, OP.add), "V")
                vector.wait_ge(mk.sem["V"], v)
            v = mk.inc(vector.tensor_copy(mk.cnt8[:], mk.lgT[:, T - 1:T]),
                       "V")
            vector.wait_ge(mk.sem["V"], v)
            mk.wait(vector, "G")
            v = mk.inc(vector.memset(mk.ccs[:], 0.0), "V")
            vector.wait_ge(mk.sem["V"], v)
            for arm in vector.switch_core_id(n=128):
                c = arm.logical % 8
                vector.tensor_copy(mk.ccs[0:E, c:c + 1], mk.cnt8[:])
            mk.inc(vector.engine_nop(), "V")
        ccs_V = mk.cnt["V"]

        @block.gpsimd
        def _(g):
            mk.wait(g, "V", ccs_V)
            mk.wait(g, "qG")
            mk.inc(g.dma_start(mk.cc_in[idx][:], mk.ccs[:]), "qG", 16)
            mk.wait(g, "qG")
            mk.inc(g.collective_compute(
                "AllReduce", OP.add,
                replica_groups=[list(range(NCORES))],
                ins=[mk.cc_in[idx].opt()], outs=[mk.cc_out[idx].opt()]), "C")
            mk.wait(g, "C")
            mk.inc(g.dma_start(mk.ccs[:], mk.cc_out[idx][:]), "qG", 16)
            mk.wait(g, "qG")
            mk.inc(g.engine_nop(), "G")
        coll_G = mk.cnt["G"]

        @block.vector
        def _(vector):
            mk.wait(vector, "G", coll_G)
            v = mk.inc(vector.tensor_mul(mk.cct[:], mk.ccs[0:E, 0:E],
                                         mk.selm[:]), "V")
            vector.wait_ge(mk.sem["V"], v)
            v = mk.inc(vector.reduce_sum(mk.off8[:], mk.cct[:], axis=AX), "V")
            vector.wait_ge(mk.sem["V"], v)
            v = mk.inc(vector.tensor_scalar(
                mk.lgT[:], mk.lgT[:], mk.off8[0:E, 0:1], None, OP.add), "V")
            vector.wait_ge(mk.sem["V"], v)
            v = mk.inc(vector.tensor_scalar(
                mk.lgT[:], mk.lgT[:], CAP + 0.5, None, OP.is_lt), "V")
            vector.wait_ge(mk.sem["V"], v)
            mk.inc(vector.tensor_mul(mk.lgT[:], mk.lgT[:], mk.mT[:]), "V")
        keepT_V = mk.cnt["V"]

        kc_V = {}
        for i in range(NCH):
            @block.tensor
            def _(tensor, i=i):
                mk.wait(tensor, "V", keepT_V if i == 0 else kc_V[i - 1])
                mk.inc(tensor.matmul(
                    mk.pmini[0:1, 0:CH], mk.onescol[0:E, 0:1],
                    mk.lgT[:, CH * i:CH * i + CH],
                    start=True, stop=True), "T")
            kr_T = mk.cnt["T"]

            @block.vector
            def _(vector, i=i, kr_T=kr_T):
                mk.wait(vector, "T", kr_T)
                mk.inc(vector.tensor_copy(mk.rowch("keep"),
                                          mk.pmini[0:1, 0:CH]), "V")
            krow_V = mk.cnt["V"]

            @block.tensor
            def _(tensor, i=i, krow_V=krow_V):
                mk.wait(tensor, "V", krow_V)
                last = None
                for tt in range(4):
                    last = tensor.transpose(
                        mk.pmini2[0:128, tt:tt + 1],
                        mk.rch[0:1, CH * mk.RCH["keep"] + 128 * tt:
                               CH * mk.RCH["keep"] + 128 * tt + 128],
                        mk.idenf[0:1, 0:1])
                mk.inc(last, "T")
            kcT = mk.cnt["T"]

            @block.vector
            def _(vector, i=i, kcT=kcT):
                mk.wait(vector, "T", kcT)
                last = None
                for tt in range(4):
                    last = vector.tensor_copy(
                        mk.col("keep", 4 * i + tt), mk.pmini2[0:128, tt:tt + 1])
                kc_V[i] = mk.inc(last, "V")

        @block.vector
        def _(vector):
            mk.wait(vector, "V", kc_V[NCH - 1])
            v = mk.inc(vector.scalar_tensor_tensor(
                mk.colslice("geff"), mk.colslice("gate"), 0.5,
                mk.colslice("keep"), OP.mult, OP.mult), "V")
            vector.wait_ge(mk.sem["V"], v)
            v = mk.inc(vector.tensor_mul(
                mk.colslice("ga"), mk.colslice("r"), mk.colslice("geff")),
                "V")
            vector.wait_ge(mk.sem["V"], v)
            mk.inc(vector.tensor_mul(
                mk.colslice("gb"), mk.colslice("mr"), mk.colslice("geff")),
                "V")
        grow_V = mk.cnt["V"]

        ld2 = {}
        rowsB_V = {}
        ab_T = {}
        ln_V = {}
        xw_st = {}
        exp_T = {}
        pp_free = {}

        @block.sync
        def _(sync):
            mk.wait(sync, "T", gate_T[NCH - 1])
            ld2[0] = load_chunk(sync, 0, mk.xw[0])
            ld2[1] = load_chunk(sync, 1, mk.xw[1])

        for i in range(NCH):
            b = i % 2

            @block.tensor
            def _(tensor, i=i):
                mk.wait(tensor, "V", grow_V)
                if i >= 1:
                    mk.wait(tensor, "V", rowsB_V[i - 1])
                    s, v = pp_free[i * NF - 1]
                    mk.wait(tensor, s, v)
                last = None
                for tt in range(4):
                    k = 4 * i + tt
                    c = 128 * tt
                    tensor.transpose(mk.pmini[0:1, c:c + 128],
                                     mk.col("ga", k), mk.idenf[:])
                    tensor.transpose(mk.pmini2[0:1, c:c + 128],
                                     mk.col("gb", k), mk.idenf[:])
                    last = tensor.transpose(mk.pp[1][0:1, c:c + 128],
                                            mk.col("geff", k), mk.idenf[:])
                mk.inc(last, "T")
            rB_T = mk.cnt["T"]

            @block.vector
            def _(vector, i=i, rB_T=rB_T):
                mk.wait(vector, "T", rB_T)
                vector.tensor_copy(mk.rowch("ga"), mk.pmini[0:1, 0:CH])
                vector.tensor_copy(mk.rowch("gb"), mk.pmini2[0:1, 0:CH])
                rowsB_V[i] = mk.inc(vector.tensor_copy(
                    mk.geffb_ch[:], mk.pp[1][0:1, 0:CH]), "V")

            @block.tensor
            def _(tensor, i=i):
                mk.wait(tensor, "V", rowsB_V[i])
                if i >= 1:
                    mk.wait(tensor, "V", ln_V[i - 1])
                tensor.matmul(mk.pq2[0][:, 0:CH], mk.onesrow[0:1, :],
                              mk.rowch("ga"), start=True, stop=True)
                ab_T[i] = mk.inc(tensor.matmul(
                    mk.pq2[1][:, 0:CH], mk.onesrow[0:1, :],
                    mk.rowch("gb"), start=True, stop=True), "T")

            @block.vector
            def _(vector, i=i, b=b):
                mk.wait(vector, *ld2[i])
                mk.wait(vector, "T", ab_T[i])
                if i >= 1:
                    mk.wait(vector, "T", exp_T[i * NF - 1])
                for f in range(NF):
                    sl = slice(CH * f, CH * f + CH)
                    vector.tensor_mul(mk.lnt[:, sl], mk.xw[b][:, sl],
                                      mk.pq2[0][:, 0:CH])
                v = mk.inc(vector.engine_nop(), "V")
                vector.wait_ge(mk.sem["V"], v)
                last = None
                for f in range(NF):
                    sl = slice(CH * f, CH * f + CH)
                    last = vector.tensor_tensor(
                        mk.lnw[:, sl], mk.lnt[:, sl], mk.pq2[1][:, 0:CH],
                        OP.subtract)
                ln_V[i] = mk.inc(last, "V")

            for mo2 in range(NF // 2):
                @block.tensor
                def _(tensor, i=i, mo2=mo2):
                    mk.wait(tensor, "V", ln_V[i])
                    if i == 0 and mo2 == 0:
                        mk.wait(tensor, "qG", wes_qG)
                    for mo in (2 * mo2, 2 * mo2 + 1):
                        gi = i * NF + mo
                        if gi - 2 >= 0 and (gi - 2) in pp_free:
                            s, v = pp_free[gi - 2]
                            mk.wait(tensor, s, v)
                        p = mk.pp[mo % 2]
                        for k in range(NF):
                            tensor.matmul(
                                p[:], mk.wes[:, D * k + 128 * mo:
                                             D * k + 128 * mo + 128],
                                mk.lnw[:, CH * k:CH * k + CH],
                                start=(k == 0), stop=False)
                        exp_T[gi] = mk.inc(tensor.matmul(
                            p[:], mk.bexps[0:1, 128 * mo:128 * mo + 128],
                            mk.geffb_ch[:], start=False, stop=True), "T")

                @block.vector
                def _(vector, i=i, b=b, mo2=mo2):
                    for mo in (2 * mo2, 2 * mo2 + 1):
                        gi = i * NF + mo
                        mk.wait(vector, "T", exp_T[gi])
                        sl = slice(CH * mo, CH * mo + CH)
                        v = mk.inc(vector.tensor_add(
                            mk.xw[b][:, sl], mk.xw[b][:, sl],
                            mk.pp[mo % 2][:]), "V")
                        pp_free[gi] = ("V", v)
            ev_V = mk.cnt["V"]

            @block.sync
            def _(sync, i=i, b=b, ev_V=ev_V):
                mk.wait(sync, "V", ev_V)
                xw_st[i] = store_chunk(sync, i, mk.xw[b])
                if i + 2 < NCH:
                    mk.wait(sync, *xw_st[i])
                    ld2[i + 2] = load_chunk(sync, i + 2, mk.xw[b])

        return [("V", mk.cnt["V"]), ("qSt", mk.cnt["qSt"]),
                ("T", mk.cnt["T"])]

    def p_attn(self, block, b, pre_wait=()):
        mk = self
        aF = self.arenaF
        aFB = self.aFB
        aB = self.arenaB

        QF, KF, VT = 0, 2048, 4096
        EP, E2O, ESO = 6208, 15424, 17984
        A2O, PTO, ATO, WSO = 20544, 25152, 25664, 45056
        PADL = 128
        EPW, E2W, ESW, A2W = 2304, 1280, 1280, 1152

        def bufB(off, w, p0=0, pn=128):
            return aFB[p0:p0 + pn, off:off + w]

        A = dict(
            Qfm=bufB(QF, 2048), Kfm=bufB(KF, 2048), Vtm=bufB(VT, 2112),
            E1P=[bufB(EP + i * EPW, EPW) for i in range(4)],
            E2b=[bufB(E2O + h * E2W, E2W) for h in range(2)],
            E2sb=[bufB(ESO + h * ESW, ESW) for h in range(2)],
            A2b=[bufB(A2O + i * A2W, A2W) for i in range(4)],
            PTb=[bufB(PTO + h * 256, 256) for h in range(2)],
            Vfmt=bufB(PTO, 512),
            A2Tb=[bufB(ATO + h * 256, 256) for h in range(2)],
            negb=[aF[:, 16384 + 16 * h:16400 + 16 * h] for h in range(2)],
            bkbp=[aF[:, 16416 + h:16417 + h] for h in range(2)],
            tmpln=aF[:, 16432:16944],
            rcbc=aF[:, 16944:17456],
            qcol=[aF[:, 17456 + 16 * h:17472 + 16 * h] for h in range(2)],
            e2cf=[aF[:, 17488 + h:17489 + h] for h in range(2)],
            nrow=aF[0:1, 17504:21600],
            rcrow=aF[0:1, 21600:22112],
            bks=aF[0:1, 22112:22144],
            ln2b=aB[:, 0:16384], aob=aB[:, 16384:32768],
            pm_bf=self.pmini[:, 0:256].bitcast(BF16),
            pm2_bf=self.pmini2[:, 0:256].bitcast(BF16),
            vt_bf=self.pq21[:, 512:768].bitcast(BF16),
            PADL=PADL, EP=EP, EPW=EPW, E2O=E2O, ESO=ESO, A2O=A2O, WSO=WSO,
            aFB=aFB, aF=aF,
        )
        A["FBH"] = A["Qfm"].tensor
        A["FBP"] = 2 * 28672

        xdb = [self.xd[128 * f:128 * f + 128, S * b:S * b + S]
               for f in range(NF)]

        bks = A["bks"]
        if b == 0:
            @block.scalar
            def _(scalar):
                mk.wait(scalar, "qG")
                for s, v in pre_wait:
                    mk.wait(scalar, s, v)
                mk.inc(scalar.activation(
                    bufB(EP, 1152, 0, 64), mk.pekTs[0:64, :], AF.Square),
                    "S")
            pk_S = mk.cnt["S"]

            @block.tensor
            def _(tensor):
                mk.wait(tensor, "S", pk_S)
                for s, v in pre_wait:
                    mk.wait(tensor, s, v)
                tensor.matmul(self.pq2[0][0:1, 0:512], mk.onesb[0:64, 0:1],
                              bufB(EP, 512, 0, 64), start=True, stop=True)
                tensor.matmul(self.pq2[0][0:1, 512:1024],
                              mk.onesb[0:64, 0:1],
                              bufB(EP + 512, 512, 0, 64),
                              start=True, stop=True)
                mk.inc(tensor.matmul(
                    self.pmini[0:1, 0:128], mk.onesb[0:64, 0:1],
                    bufB(EP + 1024, 128, 0, 64), start=True, stop=True), "T")
            pk_T = mk.cnt["T"]

            @block.vector
            def _(vector):
                mk.wait(vector, "T", pk_T)
                vector.reduce_max(bks[0:1, 2:3], self.pq2[0][0:1, 0:512],
                                  axis=AX)
                vector.reduce_max(bks[0:1, 3:4], self.pq2[0][0:1, 512:1024],
                                  axis=AX)
                v = mk.inc(vector.reduce_max(
                    bks[0:1, 4:5], self.pmini[0:1, 0:128], axis=AX), "V")
                vector.wait_ge(mk.sem["V"], v)
                mk.inc(vector.reduce_max(bks[0:1, 0:1], bks[0:1, 2:5],
                                         axis=AX), "V")
            bp_V = mk.cnt["V"]

            @block.scalar
            def _(scalar):
                mk.wait(scalar, "V", bp_V)
                mk.inc(scalar.activation(bks[0:1, 1:2], bks[0:1, 0:1],
                                         AF.Sqrt), "S")
            self.bp_S = mk.cnt["S"]

        @block.sync
        def _(sync):
            for s, v in pre_wait:
                mk.wait(sync, s, v)
            if b == 0:
                mk.wait(sync, "T")
            if mk.cnt["qL0"] > 0:
                mk.wait(sync, "qL0")
            for f in range(NF):
                mk.inc(sync.dma_start(aF[:, 2048 * f:2048 * f + 2048],
                                      xdb[f]), "qL0", 16)
        xb_q = mk.cnt["qL0"]

        st_T = {}
        bnB_V = {}
        mrB_T = {}
        lnB_V = {}
        for ci in range(4):
            for tt in range(4):
                k = 4 * ci + tt

                @block.tensor
                def _(tensor, ci=ci, tt=tt, k=k):
                    mk.wait(tensor, "qL0", xb_q)
                    if k == 0:
                        for s, v in pre_wait:
                            mk.wait(tensor, s, v)
                        mk.wait(tensor, "V")
                    if k - 2 >= 0:
                        mk.wait(tensor, "V", bnB_V[k - 2])
                    last = None
                    for f in range(NF):
                        last = tensor.transpose(
                            mk.pq2[k % 2][:, 128 * f:128 * f + 128],
                            aF[:, 2048 * f + 128 * k:
                               2048 * f + 128 * k + 128],
                            mk.idenf[:])
                    st_T[k] = mk.inc(last, "T")

                @block.vector
                def _(vector, k=k):
                    mk.wait(vector, "T", st_T[k])
                    p = mk.pq2[k % 2]
                    sm = mk.smeta[k % 2]
                    vector.bn_stats(mk.bn6[:, 0:6], p[:, 0:512])
                    v = mk.inc(vector.bn_stats(mk.bn6[:, 6:12],
                                               p[:, 512:1024]), "V")
                    bnB_V[k] = v
                    vector.wait_ge(mk.sem["V"], v)
                    v = mk.inc(vector.bn_aggr(mk.mvar[:], mk.bn6[:, 0:12]),
                               "V")
                    vector.wait_ge(mk.sem["V"], v)
                    if k - 2 >= 0:
                        mk.wait(vector, "T", mrB_T[k - 2])
                    vector.tensor_copy(sm[:, 0:1], mk.mvar[:, 0:1])
                    mk.inc(vector.tensor_scalar(
                        sm[:, 5:6], mk.mvar[:, 1:2], EPS, None, OP.add), "V")
                sq_in = mk.cnt["V"]

                @block.scalar
                def _(scalar, k=k, sq_in=sq_in):
                    mk.wait(scalar, "V", sq_in)
                    mk.inc(scalar.activation(
                        mk.smeta[k % 2][:, 6:7], mk.smeta[k % 2][:, 5:6],
                        AF.Sqrt), "S")
                sq_out = mk.cnt["S"]

                @block.vector
                def _(vector, k=k, sq_out=sq_out):
                    mk.wait(vector, "S", sq_out)
                    sm = mk.smeta[k % 2]
                    v = mk.inc(vector.reciprocal(sm[:, 1:2], sm[:, 6:7]),
                               "V")
                    vector.wait_ge(mk.sem["V"], v)
                    mk.inc(vector.tensor_mul(sm[:, 2:3], sm[:, 0:1],
                                             sm[:, 1:2]), "V")
                mr_in = mk.cnt["V"]

                @block.tensor
                def _(tensor, ci=ci, tt=tt, k=k, mr_in=mr_in):
                    mk.wait(tensor, "V", mr_in)
                    if tt == 0 and ci > 0:
                        mk.wait(tensor, "V", lnB_V[ci - 1])
                    tensor.transpose(
                        mk.pmini[0:1, 128 * tt:128 * tt + 128],
                        mk.smeta[k % 2][:, 1:2], mk.idenf[:])
                    mrB_T[k] = mk.inc(tensor.transpose(
                        mk.pmini2[0:1, 128 * tt:128 * tt + 128],
                        mk.smeta[k % 2][:, 2:3], mk.idenf[:]), "T")

            @block.vector
            def _(vector, ci=ci):
                mk.wait(vector, "T", mrB_T[4 * ci + 3])
                vector.tensor_copy(mk.rowch("r"), mk.pmini[0:1, 0:CH])
                mk.inc(vector.tensor_copy(
                    mk.rowch("mr"), mk.pmini2[0:1, 0:CH]), "V")
            rowB_V = mk.cnt["V"]

            @block.tensor
            def _(tensor, ci=ci, rowB_V=rowB_V):
                mk.wait(tensor, "V", rowB_V)
                tensor.matmul(mk.pp[0][:, 0:CH], mk.onesrow[0:1, :],
                              mk.rowch("r"), start=True, stop=True)
                mk.inc(tensor.matmul(
                    mk.pp[1][:, 0:CH], mk.onesrow[0:1, :],
                    mk.rowch("mr"), start=True, stop=True), "T")
            bcB_T = mk.cnt["T"]

            @block.vector
            def _(vector, ci=ci, bcB_T=bcB_T):
                mk.wait(vector, "T", bcB_T)
                for f in range(NF):
                    xsl = aF[:, 2048 * f + 512 * ci:
                             2048 * f + 512 * ci + 512]
                    lsl = A["ln2b"][:, 2048 * f + 512 * ci:
                                    2048 * f + 512 * ci + 512]
                    v = mk.inc(vector.tensor_mul(A["tmpln"][:], xsl,
                                                 mk.pp[0][:, 0:CH]), "V")
                    vector.wait_ge(mk.sem["V"], v)
                    v = mk.inc(vector.tensor_tensor(
                        lsl, A["tmpln"][:], mk.pp[1][:, 0:CH], OP.subtract),
                        "V")
                    vector.wait_ge(mk.sem["V"], v)
                lnB_V[ci] = mk.cnt["V"]
        A["ln2_V"] = lnB_V[3]

        @block.vector
        def _(vector):
            mk.wait(vector, "V", A["ln2_V"])
            for i in range(4):
                vector.memset(A["E1P"][i][:, :], 0.0)
                vector.memset(A["A2b"][i][:, :], 0.0)
            vector.memset(A["E2b"][0][:, :], 0.0)
            vector.memset(A["E2b"][1][:, :], 0.0)
            vector.memset(A["E2sb"][0][:, :], 0.0)
            vector.memset(A["E2sb"][1][:, :], 0.0)
            for kt in range(16):
                vector.memset(A["Vtm"][:, 132 * kt + 64:132 * kt + 65], 1.0)
                vector.memset(A["Vtm"][:, 132 * kt + 130:132 * kt + 131],
                              1.0)
            mk.inc(vector.engine_nop(), "V")
        A["pads_V"] = mk.cnt["V"]

        A["wsl_q"] = {}

        def wslice_load(g, p):
            if mk.cnt["qG"] > 0:
                mk.wait(g, "qG")
            base = WSO + (p % 2) * 3072
            last = None
            for wi, wt in enumerate((self.wqT, self.wkT, self.wvT)):
                for k in range(NF):
                    dstc = base + (wi * NF + k) * 128
                    last = mk.inc(g.dma_start(
                        aFB[:, dstc:dstc + 128],
                        wt[128 * k:128 * k + 128, 128 * p:128 * p + 128]),
                        "qG", 16)
            A["wsl_q"][p] = last
        A["wslice_load"] = wslice_load

        @block.gpsimd
        def _(g):
            for s, v in pre_wait:
                mk.wait(g, s, v)
            wslice_load(g, 0)

        A["prev_tail"] = ("V", A["pads_V"])
        A["prev_qkv_T"] = mk.cnt["T"]
        for p in range(H // 2):
            self._emit_pair(block, b, p, A)

        self._emit_wo(block, b, A)

        return [("V", mk.cnt["V"]), ("T", mk.cnt["T"]),
                ("qSt", mk.cnt["qSt"]), ("S", mk.cnt["S"]),
                ("qG", mk.cnt["qG"]), ("qA", mk.cnt["qA"]),
                ("qL0", mk.cnt["qL0"]), ("qS", mk.cnt["qS"])]

    def _emit_pair(self, block, b, p, A):
        mk = self
        Qfm, Kfm, Vtm = A["Qfm"], A["Kfm"], A["Vtm"]
        E1P, E2b, E2sb, A2b = A["E1P"], A["E2b"], A["E2sb"], A["A2b"]
        PTb, Vfmt, A2Tb = A["PTb"], A["Vfmt"], A["A2Tb"]
        negb, bkbp, qcol = A["negb"], A["bkbp"], A["qcol"]
        nrow, rcrow, bks, rcbc = A["nrow"], A["rcrow"], A["bks"], A["rcbc"]
        ln2b, aob = A["ln2b"], A["aob"]
        pm_bf, pm2_bf, vt_bf = A["pm_bf"], A["pm2_bf"], A["vt_bf"]
        PADL, EP, EPW = A["PADL"], A["EP"], A["EPW"]
        WSO, FBH, FBP = A["WSO"], A["FBH"], A["FBP"]
        aFB = A["aFB"]

        Qp = self.pq2[0][:, 0:512]
        Kp = self.pq2[0][:, 512:1024]
        Vp = self.pq2[1][:, 0:512]
        wbase = WSO + (p % 2) * 3072

        def wsl(wi, k):
            c = wbase + (wi * NF + k) * 128
            return aFB[:, c:c + 128]

        ev_V = {}
        for nc4 in range(4):
            @block.tensor
            def _(tensor, nc4=nc4):
                mk.wait(tensor, "qG", A["wsl_q"][p])
                mk.wait(tensor, "V", A["ln2_V"])
                if nc4 == 0:
                    s, v = A["prev_tail"]
                    mk.wait(tensor, s, v)
                else:
                    mk.wait(tensor, "V", ev_V[nc4 - 1])
                last = None
                for k in range(NF):
                    mv = ln2b[:, 2048 * k + 512 * nc4:
                              2048 * k + 512 * nc4 + 512]
                    tensor.matmul(Qp, wsl(0, k), mv,
                                  start=(k == 0), stop=(k == NF - 1))
                    tensor.matmul(Kp, wsl(1, k), mv,
                                  start=(k == 0), stop=(k == NF - 1))
                    last = tensor.matmul(Vp, wsl(2, k), mv,
                                         start=(k == 0), stop=(k == NF - 1))
                mk.inc(last, "T")
            qkv_T = mk.cnt["T"]

            @block.vector
            def _(vector, nc4=nc4, qkv_T=qkv_T):
                mk.wait(vector, "T", qkv_T)
                tsl = slice(512 * nc4, 512 * nc4 + 512)
                vector.tensor_copy(Qfm[:, tsl], Qp)
                vector.tensor_copy(Kfm[:, tsl], Kp)
                mk.inc(vector.tensor_copy(Vfmt[:], Vp), "V")
            vf_V = mk.cnt["V"]

            @block.tensor
            def _(tensor, nc4=nc4, vf_V=vf_V):
                mk.wait(tensor, "V", vf_V)
                last = None
                for j in range(4):
                    last = tensor.transpose(
                        vt_bf[:, 128 * j:128 * j + 128],
                        Vfmt[:, 128 * j:128 * j + 128], mk.idenb[:])
                mk.inc(last, "T")
            vtr_T = mk.cnt["T"]

            @block.vector
            def _(vector, nc4=nc4, vtr_T=vtr_T):
                mk.wait(vector, "T", vtr_T)
                last = None
                for j in range(4):
                    kt = 4 * nc4 + j
                    vector.tensor_tensor(
                        Vtm[:, 132 * kt:132 * kt + 64],
                        vt_bf[:, 128 * j:128 * j + 64],
                        mk.pev0bc[:], OP.add)
                    last = vector.tensor_tensor(
                        Vtm[:, 132 * kt + 66:132 * kt + 130],
                        vt_bf[:, 128 * j + 64:128 * j + 128],
                        mk.pev0bc[:], OP.add)
                ev_V[nc4] = mk.inc(last, "V")
        qkv_done_V = ev_V[3]
        qkv_done_T = mk.cnt["T"]

        if p + 1 < H // 2:
            @block.gpsimd
            def _(g):
                mk.wait(g, "T", qkv_done_T)
                A["wslice_load"](g, p + 1)

        for which, srcfm in (("q", Qfm), ("k", Kfm)):
            @block.scalar
            def _(scalar, which=which, srcfm=srcfm):
                mk.wait(scalar, "V", qkv_done_V)
                if which == "q":
                    s, v = A["prev_tail"]
                    mk.wait(scalar, s, v)
                else:
                    mk.wait(scalar, "V")
                mk.inc(scalar.activation(
                    E1P[0][:, 256:2304], srcfm[:], AF.Square), "S")
            sq_S = mk.cnt["S"]

            nrm_T = {}
            for nc4 in range(4):
                @block.tensor
                def _(tensor, nc4=nc4, sq_S=sq_S):
                    mk.wait(tensor, "S", sq_S)
                    if nc4 >= 1:
                        mk.wait(tensor, "V")
                    tsl = slice(256 + 512 * nc4, 256 + 512 * nc4 + 512)
                    tensor.matmul(self.pmini[0:1, 0:512],
                                  mk.onesb[0:64, 0:1],
                                  E1P[0][0:64, tsl], start=True, stop=True,
                                  tile_position=(0, 0))
                    nrm_T[nc4] = mk.inc(tensor.matmul(
                        self.pmini2[0:1, 0:512], mk.onesb[64:128, 0:1],
                        E1P[0][64:128, tsl], start=True, stop=True,
                        tile_position=(64, 0)), "T")

                @block.vector
                def _(vector, nc4=nc4):
                    mk.wait(vector, "T", nrm_T[nc4])
                    vector.tensor_copy(nrow[0:1, 512 * nc4:512 * nc4 + 512],
                                       self.pmini[0:1, 0:512])
                    mk.inc(vector.tensor_copy(
                        nrow[0:1, 2048 + 512 * nc4:2048 + 512 * nc4 + 512],
                        self.pmini2[0:1, 0:512]), "V")
            nrows_V = mk.cnt["V"]

            if which == "q":
                qc_T = {}
                for h in range(2):
                    @block.tensor
                    def _(tensor, h=h, nrows_V=nrows_V):
                        mk.wait(tensor, "V", nrows_V)
                        last = None
                        for s in range(16):
                            o = 2048 * h + 128 * s
                            last = tensor.transpose(
                                [self.pmini, self.pmini2][h][0:128,
                                                             256 + s:257 + s],
                                nrow[0:1, o:o + 128], mk.idenf[0:1, 0:1])
                        qc_T[h] = mk.inc(last, "T")

                    @block.vector
                    def _(vector, h=h):
                        mk.wait(vector, "T", qc_T[h])
                        mk.inc(vector.tensor_copy(
                            qcol[h][:],
                            [self.pmini, self.pmini2][h][0:128, 256:272]),
                            "V")
            else:
                @block.vector
                def _(vector, nrows_V=nrows_V):
                    mk.wait(vector, "V", nrows_V)
                    vector.reduce_max(bks[0:1, 8:9], nrow[0:1, 0:2048],
                                      axis=AX)
                    mk.inc(vector.reduce_max(
                        bks[0:1, 9:10], nrow[0:1, 2048:4096], axis=AX), "V")
                bkm_V = mk.cnt["V"]

                @block.scalar
                def _(scalar, bkm_V=bkm_V):
                    mk.wait(scalar, "V", bkm_V)
                    if b == 0 and p == 0:
                        mk.wait(scalar, "S", self.bp_S)
                    scalar.activation(bks[0:1, 10:11], bks[0:1, 8:9],
                                      AF.Sqrt)
                    mk.inc(scalar.activation(bks[0:1, 11:12],
                                             bks[0:1, 9:10], AF.Sqrt), "S")
                bky_S = mk.cnt["S"]

                @block.vector
                def _(vector, bky_S=bky_S):
                    mk.wait(vector, "S", bky_S)
                    vector.tensor_tensor(bks[0:1, 12:13], bks[0:1, 10:11],
                                         bks[0:1, 1:2], OP.add)
                    mk.inc(vector.tensor_tensor(
                        bks[0:1, 13:14], bks[0:1, 11:12], bks[0:1, 1:2],
                        OP.add), "V")
                bka_V = mk.cnt["V"]

                @block.tensor
                def _(tensor, bka_V=bka_V):
                    mk.wait(tensor, "V", bka_V)
                    tensor.matmul(self.pmini[0:128, 272:273],
                                  mk.onesrow[0:1, :], bks[0:1, 12:13],
                                  start=True, stop=True)
                    mk.inc(tensor.matmul(
                        self.pmini2[0:128, 272:273], mk.onesrow[0:1, :],
                        bks[0:1, 13:14], start=True, stop=True), "T")
                bkb_T = mk.cnt["T"]

                @block.vector
                def _(vector, bkb_T=bkb_T):
                    mk.wait(vector, "T", bkb_T)
                    vector.tensor_copy(bkbp[0][:],
                                       self.pmini[0:128, 272:273])
                    mk.inc(vector.tensor_copy(
                        bkbp[1][:], self.pmini2[0:128, 272:273]), "V")

        @block.scalar
        def _(scalar):
            mk.wait(scalar, "V")
            scalar.activation(qcol[0][:], qcol[0][:], AF.Sqrt)
            mk.inc(scalar.activation(qcol[1][:], qcol[1][:], AF.Sqrt), "S")
        nq_S = mk.cnt["S"]

        @block.vector
        def _(vector, nq_S=nq_S):
            mk.wait(vector, "S", nq_S)
            vector.tensor_scalar(negb[0][:], qcol[0][:], -0.125,
                                 bkbp[0][:, 0:1], OP.mult, OP.mult)
            mk.inc(vector.tensor_scalar(
                negb[1][:], qcol[1][:], -0.125,
                bkbp[1][:, 0:1], OP.mult, OP.mult), "V")
        prep_V = mk.cnt["V"]

        subst = dict(prep_V=prep_V, e1_S={}, pmul_V={}, shearA_qA={},
                     e2_S={}, last_scoreS={}, tail_V=A["pads_V"],
                     shear_q={-1: A.get("last_shear_q", 0)},
                     avstart=None)
        for sb in range(8):
            for s01 in range(2):
                self._emit_sub(block, b, p, 2 * sb + s01, A, subst)
            self._emit_tail(block, b, p, sb, A, subst)
        A["last_shear_q"] = subst["shear_q"][15]
        A["prev_tail"] = ("V", subst["tail_V"])

    def _emit_sub(self, block, b, p, qi, A, SS):
        mk = self
        PADL, EPW = A["PADL"], A["EPW"]
        FBH, FBP = A["FBH"], A["FBP"]
        q0 = 128 * qi
        si = qi % 2
        kfar = q0 - 1024 if q0 >= 1024 else 0
        NZW = q0 + 128 - kfar
        C0 = kfar - q0 + 1024
        jexp0 = max(0, 897 - q0)
        jc0 = (jexp0 // 512) * 512
        jA0 = max(0, ((897 - q0) // 128) * 128) if q0 < 897 else 0
        q1 = 128 * (qi + 1) if qi % 2 == 0 else None
        jA0n = (max(0, ((897 - q0 - 128) // 128) * 128)
                if q0 + 128 < 897 else 0) if qi % 2 == 0 else None

        Qfm, Kfm = A["Qfm"], A["Kfm"]
        E1P, E2b, E2sb, A2b = A["E1P"], A["E2b"], A["E2sb"], A["A2b"]
        negb = A["negb"]
        e2cf = A["e2cf"]
        ebuf = [E1P[2 * si + h] for h in range(2)]
        EPoff = [A["EP"] + (2 * si + h) * EPW for h in range(2)]
        a2buf = [A2b[2 * si + h] for h in range(2)]
        A2off = [A["A2O"] + (2 * si + h) * 1152 for h in range(2)]

        @block.tensor
        def _(tensor):
            mk.wait(tensor, "V", SS["prep_V"])
            mk.wait(tensor, "V", SS["tail_V"])
            if qi >= 1:
                mk.wait(tensor, "S", SS["last_scoreS"][qi - 1])
            if qi >= 2:
                mk.wait(tensor, "V", SS["pmul_V"][qi - 2])
            last = None
            for jc in range(jc0, 1152, 512):
                w = min(512, 1152 - jc)
                for h in range(2):
                    dst = (self.pq2[h][:, jc:jc + w] if jc < 1024 else
                           [self.pmini, self.pmini2][h][:, 0:128])
                    last = tensor.matmul(
                        dst, Qfm[64 * h:64 * h + 64, q0:q0 + 128],
                        mk.pekTs[64 * h:64 * h + 64, jc:jc + w],
                        start=True, stop=True,
                        tile_position=(64 * h, 0))
            mk.inc(last, "T")
        pq_T = mk.cnt["T"]

        @block.scalar
        def _(scalar):
            mk.wait(scalar, "T", pq_T)
            if SS["shear_q"].get(qi - 1):
                mk.wait(scalar, "qS", SS["shear_q"][qi - 1])
            last = None
            for h in range(2):
                if jexp0 < 1024:
                    last = scalar.activation(
                        E2b[h][:, PADL + jexp0:PADL + 1024],
                        self.pq2[h][:, jexp0:1024], AF.Exp,
                        bias=negb[h][:, qi:qi + 1], scale=0.125)
                last = scalar.activation(
                    E2b[h][:, PADL + 1024:PADL + 1025],
                    [self.pmini, self.pmini2][h][:, 0:1], AF.Exp,
                    bias=negb[h][:, qi:qi + 1], scale=0.125)
            mk.inc(last, "S")
        SS["e2_S"][qi] = mk.cnt["S"]

        @block.vector
        def _(vector):
            mk.wait(vector, "S", SS["e2_S"][qi])
            if q0 >= 1024 and SS["shear_q"].get(qi - 1):
                mk.wait(vector, "qS", SS["shear_q"][qi - 1])
            if q0 >= 1024 and SS["pmul_V"].get(qi - 1):
                mk.wait(vector, "V", SS["pmul_V"][qi - 1])
            if q0 >= 1024:
                for h in range(2):
                    v = mk.inc(vector.tensor_copy(
                        e2cf[h][:], E2b[h][:, PADL:PADL + 1]), "V")
                    vector.wait_ge(mk.sem["V"], v)
                    vector.tensor_scalar(
                        E2b[h][:, 0:PADL], mk.idenf[:], 0.0,
                        e2cf[h][:, 0:1], OP.mult, OP.add)
            mk.inc(vector.engine_nop(), "V")
        fill_V = mk.cnt["V"]

        @block.sync
        def _(sync):
            mk.wait(sync, "V", fill_V)
            if mk.cnt["qS"] > 0:
                mk.wait(sync, "qS")
            if mk.cnt["qA"] > 0:
                mk.wait(sync, "qA")
            for h in range(2):
                srcoff = A["E2O"] + h * 1280 + PADL + C0
                ap = bass.AP(FBH, srcoff, [[FBP - 1, 128], [1, NZW]])
                mk.inc(sync.dma_start(E2sb[h][:, 128:128 + NZW], ap),
                       "qS", 16)
        shear_q = mk.cnt["qS"]
        SS["shear_q"][qi] = shear_q

        prevS = None
        for ks in range(0, q0 + 128, 1024):
            wsup = min(1024, q0 + 128 - ks)

            @block.tensor
            def _(tensor, ks=ks, wsup=wsup, prevS=prevS):
                if ks == 0:
                    mk.wait(tensor, "S", SS["e2_S"][qi])
                else:
                    mk.wait(tensor, "S", prevS)
                last = None
                for kc in range(ks, ks + wsup, 512):
                    w = min(512, q0 + 128 - kc)
                    for h in range(2):
                        last = tensor.matmul(
                            self.pq2[h][:, kc - ks:kc - ks + w],
                            Qfm[64 * h:64 * h + 64, q0:q0 + 128],
                            Kfm[64 * h:64 * h + 64, kc:kc + w],
                            start=True, stop=True,
                            tile_position=(64 * h, 0))
                mk.inc(last, "T")
            sc_T = mk.cnt["T"]

            @block.scalar
            def _(scalar, ks=ks, wsup=wsup, sc_T=sc_T):
                mk.wait(scalar, "T", sc_T)
                last = None
                for h in range(2):
                    last = scalar.activation(
                        ebuf[h][:, 256 + ks:256 + ks + wsup],
                        self.pq2[h][:, 0:wsup], AF.Exp, scale=0.125)
                mk.inc(last, "S")
            prevS = mk.cnt["S"]
        SS["last_scoreS"][qi] = prevS

        @block.vector
        def _(vector):
            mk.wait(vector, "S", SS["last_scoreS"][qi])
            mk.wait(vector, "qS", shear_q)
            last = None
            for h in range(2):
                if kfar > 0:
                    vector.tensor_scalar(
                        ebuf[h][:, 256:256 + kfar],
                        ebuf[h][:, 256:256 + kfar],
                        e2cf[h][:, 0:1], None, OP.mult)
                last = vector.tensor_mul(
                    ebuf[h][:, 256 + kfar:256 + q0 + 128],
                    ebuf[h][:, 256 + kfar:256 + q0 + 128],
                    E2sb[h][:, 128:128 + NZW])
                if qi % 2 == 0:
                    last = vector.memset(
                        ebuf[h][:, 256 + q0 + 128:256 + q0 + 256], 0.0)
            mk.inc(last, "V")
        SS["pmul_V"][qi] = mk.cnt["V"]

    def _emit_tail(self, block, b, p, sb, A, SS):
        mk = self
        q1 = 128 * (2 * sb + 1)
        kcN = (q1 + 128) // 128
        jA0b = max(0, ((897 - q1) // 128) * 128) if q1 < 897 else 0
        jcs = list(range(jA0b // 128, 9))
        E1P, A2b, PTb, A2Tb = A["E1P"], A["A2b"], A["PTb"], A["A2Tb"]
        E2sb = A["E2sb"]
        Vtm, aob, rcbc, rcrow = A["Vtm"], A["aob"], A["rcbc"], A["rcrow"]
        pm_bf, pm2_bf = A["pm_bf"], A["pm2_bf"]
        FBH, FBP = A["FBH"], A["FBP"]

        prevEv = {}
        for s01 in range(2):
            qi = 2 * sb + s01
            q0 = 128 * qi
            kfar = q0 - 1024 if q0 >= 1024 else 0
            NZW = q0 + 128 - kfar
            jA0 = max(0, ((897 - q0) // 128) * 128) if q0 < 897 else 0

            @block.tensor
            def _(tensor, s01=s01, qi=qi, kfar=kfar, NZW=NZW):
                mk.wait(tensor, "V", SS["pmul_V"][qi])
                mk.wait(tensor, "S", SS["last_scoreS"][2 * sb + 1])
                if s01 == 1:
                    mk.wait(tensor, "V", prevEv[0])
                last = None
                for h in range(2):
                    src = E1P[2 * s01 + h]
                    for c in range(0, NZW, 512):
                        w = min(512, NZW - c)
                        dst = (self.pq2[h][:, c:c + w] if c < 1024 else
                               [self.pmini, self.pmini2][h][:, 0:w])
                        last = tensor.matmul(
                            dst, mk.antib[:],
                            src[:, 256 + kfar + c:256 + kfar + c + w],
                            start=True, stop=True)
                mk.inc(last, "T")
            flip_T = mk.cnt["T"]

            @block.vector
            def _(vector, s01=s01, NZW=NZW, flip_T=flip_T):
                mk.wait(vector, "T", flip_T)
                mk.wait(vector, "V", SS["pmul_V"][2 * sb + 1])
                if mk.cnt["qA"] > 0:
                    mk.wait(vector, "qA")
                last = None
                for h in range(2):
                    w0 = min(NZW, 1024)
                    last = vector.tensor_copy(
                        E2sb[h][:, 128:128 + w0], self.pq2[h][:, 0:w0])
                    if NZW > 1024:
                        last = vector.tensor_copy(
                            E2sb[h][:, 1152:128 + NZW],
                            [self.pmini, self.pmini2][h][:, 0:NZW - 1024])
                prevEv[s01] = mk.inc(last, "V")

            @block.scalar
            def _(scalar, s01=s01, q0=q0, kfar=kfar, jA0=jA0):
                mk.wait(scalar, "V", prevEv[s01])
                if mk.cnt["qA"] > 0:
                    mk.wait(scalar, "qA")
                for h in range(2):
                    srcoff = (A["ESO"] + h * 1280 + 128 + jA0 + q0 + 127
                              - 1024 - kfar)
                    ap = bass.AP(FBH, srcoff,
                                 [[FBP - 1, 128], [1, 1025 - jA0]])
                    mk.inc(scalar.dma_start(
                        A2b[2 * s01 + h][:, jA0:1025], ap), "qA", 16)
        shearA_q = mk.cnt["qA"]

        jA0e = (max(0, ((897 - 256 * sb) // 128) * 128)
                if 256 * sb < 897 else 0)

        @block.vector
        def _(vector):
            if jA0b < jA0e:
                mk.wait(vector, "V", SS["tail_V"])
                for h in range(2):
                    vector.memset(A2b[h][:, jA0b:jA0e], 0.0)
            mk.inc(vector.engine_nop(), "V")
        gapz_V = mk.cnt["V"]

        avps = [self.pp[0][0:65, 0:256], self.pp[1][0:65, 0:256]]
        evac_V = {}
        avT = {}
        for kc in range(kcN):
            @block.tensor
            def _(tensor, kc=kc):
                if kc == 0:
                    mk.wait(tensor, "V", SS["pmul_V"][2 * sb + 1])
                    mk.wait(tensor, "V", SS["tail_V"])
                    mk.wait(tensor, "V", prevEv[1])
                if kc >= 1:
                    mk.wait(tensor, "V", evac_V[kc - 1])
                last = None
                for h in range(2):
                    pdst = [pm_bf, pm2_bf][h]
                    for s01 in range(2):
                        last = tensor.transpose(
                            pdst[:, 128 * s01:128 * s01 + 128],
                            E1P[2 * s01 + h][:, 256 + 128 * kc:
                                             256 + 128 * kc + 128],
                            mk.idenb[:])
                mk.inc(last, "T")
            tr_T = mk.cnt["T"]

            @block.vector
            def _(vector, kc=kc, tr_T=tr_T):
                mk.wait(vector, "T", tr_T)
                vector.tensor_copy(PTb[0][:], pm_bf[:, 0:256])
                evac_V[kc] = mk.inc(vector.tensor_copy(PTb[1][:],
                                                       pm2_bf[:, 0:256]),
                                    "V")

            @block.tensor
            def _(tensor, kc=kc):
                mk.wait(tensor, "V", evac_V[kc])
                last = None
                for h in range(2):
                    last = tensor.matmul(
                        avps[h], Vtm[:, 132 * kc + 66 * h:
                                     132 * kc + 66 * h + 65],
                        PTb[h][:], start=(kc == 0), stop=False,
                        skip_group_check=True)
                avT[kc] = mk.inc(last, "T")

        a2e_V = {}
        for ji, jc in enumerate(jcs):
            @block.tensor
            def _(tensor, ji=ji, jc=jc):
                if ji == 0:
                    mk.wait(tensor, "qA", shearA_q)
                    mk.wait(tensor, "V", evac_V[kcN - 1])
                if ji >= 1:
                    mk.wait(tensor, "V", a2e_V[ji - 1])
                last = None
                for h in range(2):
                    pdst = [pm_bf, pm2_bf][h]
                    for s01 in range(2):
                        last = tensor.transpose(
                            pdst[:, 128 * s01:128 * s01 + 128],
                            A2b[2 * s01 + h][:, 128 * jc:128 * jc + 128],
                            mk.antib[:])
                mk.inc(last, "T")
            a2t_T = mk.cnt["T"]

            @block.vector
            def _(vector, ji=ji, a2t_T=a2t_T):
                mk.wait(vector, "T", a2t_T)
                vector.tensor_copy(A2Tb[0][:], pm_bf[:, 0:256])
                a2e_V[ji] = mk.inc(vector.tensor_copy(A2Tb[1][:],
                                                      pm2_bf[:, 0:256]),
                                   "V")

            @block.tensor
            def _(tensor, ji=ji, jc=jc):
                mk.wait(tensor, "V", a2e_V[ji])
                last = None
                for h in range(2):
                    last = tensor.matmul(
                        avps[h][0:64, :], mk.pevs[:, 64 * jc:64 * jc + 64],
                        A2Tb[h][:], start=False, stop=(ji == len(jcs) - 1),
                        skip_group_check=True)
                mk.inc(last, "T")
        pev_T = mk.cnt["T"]

        @block.vector
        def _(vector):
            mk.wait(vector, "T", pev_T)
            vector.reciprocal(rcrow[0:1, 0:256], self.pp[0][64:65, 0:256])
            mk.inc(vector.reciprocal(rcrow[0:1, 256:512],
                                     self.pp[1][64:65, 0:256]), "V")
        rc_V = mk.cnt["V"]

        @block.tensor
        def _(tensor, rc_V=rc_V):
            mk.wait(tensor, "V", rc_V)
            tensor.matmul(self.pmini[:, 256:512], mk.onesrow[0:1, :],
                          rcrow[0:1, 0:256], start=True, stop=True)
            mk.inc(tensor.matmul(
                self.pmini2[:, 256:512], mk.onesrow[0:1, :],
                rcrow[0:1, 256:512], start=True, stop=True), "T")
        rcb_T = mk.cnt["T"]

        @block.vector
        def _(vector, sb=sb, rcb_T=rcb_T):
            mk.wait(vector, "T", rcb_T)
            v = mk.inc(vector.tensor_copy(rcbc[:, 0:256],
                                          self.pmini[:, 256:512]), "V")
            mk.inc(vector.tensor_copy(rcbc[:, 256:512],
                                      self.pmini2[:, 256:512]), "V")
            vector.wait_ge(mk.sem["V"], mk.cnt["V"])
            ao = 2048 * p + 256 * sb
            vector.tensor_mul(aob[0:64, ao:ao + 256],
                              self.pp[0][0:64, 0:256], rcbc[0:64, 0:256])
            mk.inc(vector.tensor_mul(
                aob[64:128, ao:ao + 256], self.pp[1][0:64, 0:256],
                rcbc[64:128, 256:512]), "V")
        SS["tail_V"] = mk.cnt["V"]

        if getattr(self, "debug_dump", False) and b == 0 and p == 0 \
                and sb in (0, 7):
            E1Pb, E2bb, A2bb = A["E1P"], A["E2b"], A["A2b"]
            Qfm, Kfm, Vtm = A["Qfm"], A["Kfm"], A["Vtm"]
            tv = SS["tail_V"]
            base = 0 if sb == 0 else 8448
            ao_d = 2048 * p + 256 * sb

            @block.vector
            def _(vector, tv=tv):
                mk.wait(vector, "V", tv)
                mk.inc(vector.engine_nop(), "V")
            dv = mk.cnt["V"]

            @block.gpsimd
            def _(g, dv=dv, base=base):
                mk.wait(g, "V", dv)
                if mk.cnt["qG"] > 0:
                    mk.wait(g, "qG")
                mk.inc(g.dma_start(self.dbg[:, base + 0:base + 2304],
                                   E1Pb[1][:, :]), "qG", 16)
                mk.inc(g.dma_start(self.dbg[:, base + 2304:base + 3584],
                                   E2bb[1][:, :]), "qG", 16)
                mk.inc(g.dma_start(self.dbg[:, base + 3584:base + 4736],
                                   A2bb[1][:, :]), "qG", 16)
                mk.inc(g.dma_start(self.dbg[:, base + 4736:base + 6784],
                                   Qfm[:, :]), "qG", 16)
                mk.inc(g.dma_start(self.dbg[:, base + 6784:base + 7040],
                                   A["E2sb"][1][:, 0:256]), "qG", 16)
                mk.inc(g.dma_start(self.dbg[:, base + 7040:base + 7304],
                                   Vtm[:, 0:264]), "qG", 16)
                mk.inc(g.dma_start(self.dbg[:, base + 7304:base + 7560],
                                   A["aob"][:, ao_d:ao_d + 256]), "qG", 16)
                mk.inc(g.dma_start(self.dbg[:, base + 7560:base + 7816],
                                   A["PTb"][1][:, :]), "qG", 16)
                mk.inc(g.dma_start(self.dbg[0:1, base + 7816:base + 8328],
                                   A["rcrow"][0:1, 0:512]), "qG", 16)
                mk.wait(g, "qG")
                mk.inc(g.engine_nop(), "G")
            dg = mk.cnt["G"]

            @block.vector
            def _(vector, dg=dg):
                mk.wait(vector, "G", dg)
                mk.inc(vector.engine_nop(), "V")
            SS["tail_V"] = mk.cnt["V"]

    def _emit_wo(self, block, b, A):
        mk = self
        aob = A["aob"]
        aF = self.arenaF
        aFB = self.aFB
        WSO = A["WSO"]

        xc = [aF[:, 0:4096], aF[:, 4096:8192]]
        xdb = [self.xd[128 * f:128 * f + 128, S * b:S * b + S]
               for f in range(NF)]

        wo_q = {}

        @block.gpsimd
        def _(g):
            s, v = A["prev_tail"]
            mk.wait(g, s, v)
            for half in range(2):
                if mk.cnt["qG"] > 0:
                    mk.wait(g, "qG")
                last = None
                for k in range(NF):
                    for oc in range(4):
                        c = WSO + 4096 * half + (k * 4 + oc) * 128
                        last = mk.inc(g.dma_start(
                            aFB[:, c:c + 128],
                            self.woT[128 * k:128 * k + 128,
                                     128 * (4 * half + oc):
                                     128 * (4 * half + oc) + 128]),
                            "qG", 16)
                wo_q[half] = last

        ldx = {}

        @block.sync
        def _(sync):
            s, v = A["prev_tail"]
            mk.wait(sync, s, v)
            for ncx in range(2):
                if mk.cnt["qL0"] > 0:
                    mk.wait(sync, "qL0")
                last = None
                for f in range(NF):
                    last = mk.inc(sync.dma_start(
                        xc[ncx][:, 512 * f:512 * f + 512],
                        xdb[f][:, 512 * ncx:512 * ncx + 512]), "qL0", 16)
                ldx[ncx] = last

        add_V = {}
        for ncx in range(4):
            bx = ncx % 2
            for oc2 in range(NF // 2):
                @block.tensor
                def _(tensor, ncx=ncx, oc2=oc2):
                    if ncx == 0 and oc2 == 0:
                        s, v = A["prev_tail"]
                        mk.wait(tensor, s, v)
                        mk.wait(tensor, "qG", wo_q[0])
                    if ncx == 0 and oc2 == 2:
                        mk.wait(tensor, "qG", wo_q[1])
                    for oc in (2 * oc2, 2 * oc2 + 1):
                        gi = ncx * NF + oc
                        if gi - 2 >= 0 and (gi - 2) in add_V:
                            mk.wait(tensor, "V", add_V[gi - 2])
                        half, oc4 = divmod(oc, 4)
                        po = self.pq2[oc % 2][:, 0:512]
                        last = None
                        for k in range(NF):
                            c = WSO + 4096 * half + (k * 4 + oc4) * 128
                            last = tensor.matmul(
                                po, aFB[:, c:c + 128],
                                aob[:, 2048 * k + 512 * ncx:
                                    2048 * k + 512 * ncx + 512],
                                start=(k == 0), stop=(k == NF - 1))
                        mk.inc(last, "T")
                woT2 = mk.cnt["T"]

                @block.vector
                def _(vector, ncx=ncx, bx=bx, oc2=oc2, woT2=woT2):
                    mk.wait(vector, "T", woT2)
                    mk.wait(vector, "qL0", ldx[ncx])
                    for oc in (2 * oc2, 2 * oc2 + 1):
                        gi = ncx * NF + oc
                        v = mk.inc(vector.tensor_add(
                            xc[bx][:, 512 * oc:512 * oc + 512],
                            xc[bx][:, 512 * oc:512 * oc + 512],
                            self.pq2[oc % 2][:, 0:512]), "V")
                        add_V[gi] = v
            chunk_add = mk.cnt["V"]

            @block.sync
            def _(sync, ncx=ncx, bx=bx, chunk_add=chunk_add):
                mk.wait(sync, "V", chunk_add)
                if mk.cnt["qSt"] > 0:
                    mk.wait(sync, "qSt")
                for f in range(NF):
                    mk.inc(sync.dma_start(
                        xdb[f][:, 512 * ncx:512 * ncx + 512],
                        xc[bx][:, 512 * f:512 * f + 512]), "qSt", 16)
                if ncx + 2 < 4:
                    mk.wait(sync, "qSt")
                    mk.wait(sync, "qL0")
                    last = None
                    for f in range(NF):
                        last = mk.inc(sync.dma_start(
                            xc[bx][:, 512 * f:512 * f + 512],
                            xdb[f][:, 512 * (ncx + 2):
                                   512 * (ncx + 2) + 512]), "qL0", 16)
                    ldx[ncx + 2] = last

    def p_finish(self, block):
        mk = self

        @block.sync
        def _(sync):
            for q in ("qS", "qG", "qA", "qL0", "qL1", "qSt"):
                if mk.cnt[q] > 0:
                    mk.wait(sync, q)


def build_program(stop_after="full"):
    nc = bass.Bass("TRN2", target_bir_lowering=False)
    with ExitStack() as st:
        mk = MK(nc, st, stop_after)
        block = st.enter_context(nc.Block())
        mk.p_init(block)
        if stop_after == "ffn1":
            mk.p_ffn(block, 0, mk.xin, mk.out)
        else:
            w = mk.p_ffn(block, 0, mk.xin, mk.xd)
            if stop_after in ("full", "attn"):
                w = mk.p_attn(block, 0, pre_wait=w)
                w = mk.p_attn(block, 1, pre_wait=w)
            if stop_after == "attn":
                @block.sync
                def _(sync):
                    for s, v in w:
                        mk.wait(sync, s, v)
                    mk.inc(sync.dma_start(mk.out[:, :], mk.xd[:, :]),
                           "qSt", 16)
                    mk.wait(sync, "qSt")
            else:
                mk.p_ffn(block, 1, mk.xd, mk.out, pre_wait=w)
        mk.p_finish(block)
    return nc


def make_in_maps(L0_attn_x, L0_attn_Wq, L0_attn_Wk, L0_attn_Wv, L0_attn_Wo,
                 L0_attn_pek, L0_attn_pev, L0_ffn_w_gate, L0_ffn_w_expert,
                 L0_ffn_b_expert):
    x = np.asarray(L0_attn_x, dtype=np.float32)
    pek = np.asarray(L0_attn_pek, np.float32)
    pev = np.asarray(L0_attn_pev, np.float32)
    pekT = np.zeros((128, JPAD), np.float32)
    pekT[0:64, 0:JW] = pek[0:JW].T
    pekT[64:128, 0:JW] = pek[0:JW].T
    pevp = np.zeros((JPAD, HD), np.float32)
    pevp[0:JW] = pev[0:JW]

    common = {
        "wqT": np.ascontiguousarray(np.asarray(L0_attn_Wq, np.float32).T),
        "wkT": np.ascontiguousarray(np.asarray(L0_attn_Wk, np.float32).T),
        "wvT": np.ascontiguousarray(np.asarray(L0_attn_Wv, np.float32).T),
        "woT": np.ascontiguousarray(np.asarray(L0_attn_Wo, np.float32).T),
        "wed": np.ascontiguousarray(np.asarray(L0_ffn_w_expert, np.float32)),
        "wgd": np.ascontiguousarray(np.asarray(L0_ffn_w_gate, np.float32)),
        "pekTd": pekT, "pevd": pevp,
        "bexpd": np.ascontiguousarray(
            np.asarray(L0_ffn_b_expert, np.float32).reshape(1, D)),
        "idend": np.eye(128, dtype=np.float32),
        "antid": np.ascontiguousarray(np.eye(128, dtype=np.float32)[::-1]),
        "iota8d": np.arange(E, dtype=np.float32).reshape(E, 1),
    }
    in_maps = []
    for c in range(NCORES):
        xb = x[2 * c:2 * c + 2].reshape(T, D)
        sel = np.zeros((E, E), np.float32)
        sel[:, 0:c] = 1.0
        m = dict(common)
        m["xin"] = np.ascontiguousarray(xb.T)
        m["selmatd"] = sel
        in_maps.append(m)
    return in_maps


def kernel(**inputs):
    in_maps = make_in_maps(**inputs)
    nc = build_program("full")
    res = run_bass_kernel_spmd(nc, in_maps, core_ids=list(range(NCORES)))
    outs = []
    for c in range(NCORES):
        ofm = np.asarray(res.results[c]["out"])
        outs.append(ofm.T.reshape(B, S, D))
    return np.concatenate(outs, axis=0)


# version 02 (final)
# speedup vs baseline: 1.0197x; 1.0197x over previous
import numpy as np
import ml_dtypes
from contextlib import ExitStack

import concourse.bass as bass
import concourse.mybir as mybir
from concourse.bass_utils import run_bass_kernel_spmd

F32 = mybir.dt.float32
BF16 = mybir.dt.bfloat16
U32 = mybir.dt.uint32
AX = mybir.AxisListType.X
OP = mybir.AluOpType
AF = mybir.ActivationFunctionType

NCORES = 8
D = 1024
H = 16
HD = 64
B = 2
S = 2048
T = B * S
NF = D // 128
E = 8
CAP = 100
JW = 1025
JPAD = 1152
EPS = 1e-5
CH = 512
NCH = T // CH
DEBUG = False


class MK:
    def __init__(self, nc, st, stop_after="full"):
        self.nc = nc
        self.st = st
        self.stop_after = stop_after
        self.sem = {}
        self.cnt = {}
        self.mk_sems("qS", "qA", "qG", "qL0", "qL1", "qSt", "T", "V", "S", "G", "C")
        self.mk_dram()
        self.mk_sbuf()
        self.mk_psum()

    def mk_sems(self, *names):
        for n in names:
            self.sem[n] = self.st.enter_context(self.nc.semaphore(n))
            self.cnt[n] = 0

    def inc(self, instr, name, n=1):
        instr.then_inc(self.sem[name], n)
        self.cnt[name] += n
        return self.cnt[name]

    def wait(self, eng, name, v=None):
        eng.wait_ge(self.sem[name], self.cnt[name] if v is None else v)

    def mk_dram(self):
        nc = self.nc

        def din(name, shape, dtype=F32):
            setattr(self, name,
                    nc.dram_tensor(name, shape, dtype, kind="ExternalInput").ap())
        din("xin", [D, T])
        din("wqT", [D, D]); din("wkT", [D, D]); din("wvT", [D, D]); din("woT", [D, D])
        din("wed", [D, D]); din("wgd", [D, E])
        din("pekTd", [128, JPAD])
        din("pevd", [JPAD, HD])
        din("bexpd", [1, D])
        din("idend", [128, 128])
        din("antid", [128, 128])
        din("iota8d", [E, 1])
        din("selmatd", [E, E])
        self.out = nc.dram_tensor("out", [D, T], F32, kind="ExternalOutput").ap()
        if DEBUG:
            self.dbg = nc.dram_tensor("dbg", [128, 16896], F32,
                                      kind="ExternalOutput").ap()
        self.xd = nc.dram_tensor("xd", [D, T], F32, kind="Internal").ap()
        self.cc_in = [nc.dram_tensor(f"ccin{i}", [128, 128], F32,
                                     kind="Internal").ap() for i in range(2)]
        self.cc_out = [nc.dram_tensor(f"ccout{i}", [128, 128], F32,
                                      kind="Internal").ap() for i in range(2)]

    def mk_sbuf(self):
        def sb(name, shape, dtype):
            setattr(self, name, self.st.enter_context(
                self.nc.sbuf_tensor(name, shape, dtype)))
        sb("idenf", [128, 128], F32)
        sb("idenb", [128, 128], BF16)
        sb("onescol", [128, 1], F32)
        sb("onesb", [128, 1], BF16)
        sb("onesrow", [1, 128], F32)
        sb("onesP", [65, 128], F32)
        sb("wgs", [128, NF * E], F32)
        sb("wgcs", [E, 1], F32)
        sb("bexps", [1, D], BF16)
        sb("pekTs", [128, JPAD], BF16)
        sb("pevs", [128, (JPAD // 128) * HD], BF16)
        sb("onesbr", [1, 128], BF16)
        sb("antib", [128, 128], BF16)
        sb("pev0r", [1, HD], BF16)
        sb("pev0bc", [128, HD], BF16)
        sb("iotac", [E, 1], F32)
        sb("selm", [E, E], F32)
        sb("colmeta", [128, 8 * (T // 128)], F32)
        sb("rch", [1, 6 * CH], F32)
        sb("geffb_ch", [1, CH], BF16)
        sb("zerE", [E, CH], F32)
        sb("cnt8", [E, 1], F32)
        sb("off8", [E, 1], F32)
        sb("ccs", [128, 128], F32)
        sb("cct", [E, E], F32)
        sb("tm_lg", [128, E], F32)
        sb("tm_mx8", [128, E], F32)
        sb("tm_exp", [128, E], F32)
        sb("tm_idx", [128, 8], U32)
        sb("smeta0", [128, 8], F32)
        sb("smeta1", [128, 8], F32)
        sb("ameta", [128, 4], F32)
        sb("bmeta", [128, 2], F32)
        sb("bn6", [128, 12], F32)
        sb("mvar", [128, 2], F32)
        self.smeta = [self.smeta0, self.smeta1]
        self.CM = {"r": 0, "mr": 1, "gate": 2, "keep": 3, "geff": 4,
                   "ga": 5, "gb": 6}
        self.RCH = {"r": 0, "mr": 1, "ga": 2, "gb": 3, "best": 4, "keep": 5}
        sb("arenaF", [128, 28672], F32)
        sb("arenaB", [128, 32768], BF16)
        aF = self.arenaF
        self.aFB = aF[:].bitcast(BF16)
        self.xw0 = aF[:, 0:4096]
        self.xw1 = aF[:, 4096:8192]
        self.lnt = aF[:, 8192:12288]
        self.lgT = aF[0:E, 12288:12288 + T]
        self.mT = aF[0:E, 16384:16384 + T]
        self.lnw = self.arenaB[:, 0:4096]
        self.wes = self.arenaB[:, 4096:12288]
        self.xw = [self.xw0, self.xw1]

    def col(self, name, tt):
        return self.colmeta[:, 8 * tt + self.CM[name]:8 * tt + self.CM[name] + 1]

    def colslice(self, name):
        import concourse.bass as _b
        NT = T // 128
        return _b.AP(self.colmeta, self.CM[name], [[8 * NT, 128], [8, NT]])

    def rowch(self, name, w=CH):
        o = CH * self.RCH[name]
        return self.rch[0:1, o:o + w]

    def mk_psum(self):
        def ps(name, shape, dtype=F32):
            setattr(self, name, self.st.enter_context(
                self.nc.psum_tensor(name, shape, dtype)))
        ps("pq20", [128, 1024])
        ps("pq21", [128, 1024])
        ps("pp0", [128, 512])
        ps("pp1", [128, 512])
        ps("pmini", [128, 512])
        ps("pmini2", [128, 512])
        self.pq2 = [self.pq20, self.pq21]
        self.pp = [self.pp0, self.pp1]

    def p_init(self, block):
        mk = self

        @block.sync
        def _(sync):
            mk.inc(sync.dma_start(mk.idenf[:], mk.idend[:]), "qS", 16)
            for f in range(NF):
                mk.inc(sync.dma_start(mk.wgs[:, E * f:E * f + E],
                                      mk.wgd[128 * f:128 * f + 128, :]),
                       "qS", 16)
            mk.inc(sync.dma_start(mk.iotac[:], mk.iota8d[:]), "qS", 16)
            mk.inc(sync.dma_start(mk.selm[:], mk.selmatd[:]), "qS", 16)

        @block.gpsimd
        def _(g):
            mk.inc(g.dma_start(mk.pekTs[:], mk.pekTd[:]), "qG", 16)
            for c in range(JPAD // 128):
                mk.inc(g.dma_start(mk.pevs[:, HD * c:HD * c + HD],
                                   mk.pevd[128 * c:128 * c + 128, :]),
                       "qG", 16)
            mk.inc(g.dma_start(mk.bexps[:], mk.bexpd[:]), "qG", 16)
            mk.inc(g.dma_start(mk.pev0r[:], mk.pevd[0:1, :]), "qG", 16)
            mk.inc(g.dma_start(mk.antib[:], mk.antid[:]), "qG", 16)

        @block.vector
        def _(vector):
            mk.wait(vector, "qS", 176)
            vector.tensor_copy(mk.idenb[:], mk.idenf[:])
            vector.memset(mk.onescol[:], 1.0)
            vector.memset(mk.onesb[:], 1.0)
            vector.memset(mk.onesrow[:], 1.0)
            vector.memset(mk.onesbr[:], 1.0)
            vector.memset(mk.onesP[:], 1.0)
            vector.memset(mk.zerE[:], 0.0)
            mk.inc(vector.memset(mk.ccs[:], 0.0), "V")
        self.init_V = self.cnt["V"]

        @block.tensor
        def _(tensor):
            mk.wait(tensor, "qS", 176)
            mk.wait(tensor, "V", self.init_V)
            last = None
            for k in range(NF):
                last = tensor.matmul(
                    mk.pmini[0:E, 0:1], mk.wgs[:, E * k:E * k + E],
                    mk.onescol[:, 0:1], start=(k == 0), stop=(k == NF - 1))
            mk.inc(last, "T")

        @block.vector
        def _(vector):
            mk.wait(vector, "T")
            mk.inc(vector.tensor_copy(mk.wgcs[:], mk.pmini[0:E, 0:1]), "V")

        @block.tensor
        def _(tensor):
            mk.wait(tensor, "qG")
            mk.wait(tensor, "V", mk.cnt["V"])
            mk.inc(tensor.matmul(mk.pmini[:, 128:128 + HD],
                                 mk.onesbr[0:1, :], mk.pev0r[0:1, :],
                                 start=True, stop=True), "T")

        @block.vector
        def _(vector):
            mk.wait(vector, "T")
            mk.wait(vector, "qG")
            v = mk.inc(vector.tensor_copy(mk.pev0bc[:],
                                          mk.pmini[:, 128:128 + HD]), "V")
            vector.wait_ge(mk.sem["V"], v)
            last = None
            for c in range(JPAD // 128):
                last = vector.tensor_tensor(
                    mk.pevs[:, HD * c:HD * c + HD],
                    mk.pevs[:, HD * c:HD * c + HD], mk.pev0bc[:],
                    OP.subtract)
            mk.inc(last, "V")

    def p_ffn(self, block, idx, src, dst, pre_wait=()):
        mk = self

        def load_chunk(sync, i, buf):
            q = f"qL{i % 2}"
            if mk.cnt[q] > 0:
                mk.wait(sync, q)
            last = None
            for f in range(NF):
                last = mk.inc(sync.dma_start(
                    buf[:, CH * f:CH * f + CH],
                    src[128 * f:128 * f + 128, CH * i:CH * i + CH]), q, 16)
            return (q, last)

        def store_chunk(sync, i, buf):
            if mk.cnt["qSt"] > 0:
                mk.wait(sync, "qSt")
            last = None
            for f in range(NF):
                last = mk.inc(sync.dma_start(
                    dst[128 * f:128 * f + 128, CH * i:CH * i + CH],
                    buf[:, CH * f:CH * f + CH]), "qSt", 16)
            return ("qSt", last)

        ld = {}
        bn_V = {}
        recip_V = {}
        mrT = {}
        gate_T = {}
        fix_V = {}

        @block.sync
        def _(sync):
            for s, v in pre_wait:
                mk.wait(sync, s, v)
            ld[0] = load_chunk(sync, 0, mk.xw[0])
            ld[1] = load_chunk(sync, 1, mk.xw[1])

        @block.gpsimd
        def _(g):
            for s, v in pre_wait:
                mk.wait(g, s, v)
            if mk.cnt["qG"] > 0:
                mk.wait(g, "qG")
            for k in range(NF):
                mk.inc(g.dma_start(mk.wes[:, D * k:D * k + D],
                                   mk.wed[128 * k:128 * k + 128, :]),
                       "qG", 16)
        wes_qG = mk.cnt["qG"]

        for i in range(NCH):
            b = i % 2
            for tt in range(4):
                k = 4 * i + tt

                @block.tensor
                def _(tensor, i=i, b=b, tt=tt, k=k):
                    mk.wait(tensor, *ld[i])
                    if k - 2 >= 0:
                        mk.wait(tensor, "V", bn_V[k - 2])
                    if k <= 1:
                        for s, v in pre_wait:
                            mk.wait(tensor, s, v)
                    last = None
                    for f in range(NF):
                        last = tensor.transpose(
                            mk.pq2[k % 2][:, 128 * f:128 * f + 128],
                            mk.xw[b][:, CH * f + 128 * tt:
                                     CH * f + 128 * tt + 128],
                            mk.idenf[:])
                    mk.inc(last, "T")
                tile_T = mk.cnt["T"]

                @block.vector
                def _(vector, k=k, tt=tt, i=i, tile_T=tile_T):
                    mk.wait(vector, "T", tile_T)
                    p = mk.pq2[k % 2]
                    sm = mk.smeta[k % 2]
                    vector.bn_stats(mk.bn6[:, 0:6], p[:, 0:512])
                    v = mk.inc(vector.bn_stats(mk.bn6[:, 6:12],
                                               p[:, 512:1024]), "V")
                    bn_V[k] = v
                    vector.wait_ge(mk.sem["V"], v)
                    v = mk.inc(vector.bn_aggr(mk.mvar[:], mk.bn6[:, 0:12]),
                               "V")
                    vector.wait_ge(mk.sem["V"], v)
                    if k - 2 >= 0:
                        mk.wait(vector, "T", mrT[k - 2])
                    vector.tensor_copy(sm[:, 0:1], mk.mvar[:, 0:1])
                    mk.inc(vector.tensor_scalar(
                        sm[:, 5:6], mk.mvar[:, 1:2], EPS, None, OP.add), "V")
                sq_in = mk.cnt["V"]

                @block.scalar
                def _(scalar, k=k, sq_in=sq_in):
                    mk.wait(scalar, "V", sq_in)
                    mk.inc(scalar.activation(
                        mk.smeta[k % 2][:, 6:7], mk.smeta[k % 2][:, 5:6],
                        AF.Sqrt), "S")
                sq_out = mk.cnt["S"]

                @block.vector
                def _(vector, k=k, tt=tt, i=i, sq_out=sq_out):
                    mk.wait(vector, "S", sq_out)
                    sm = mk.smeta[k % 2]
                    v = mk.inc(vector.reciprocal(sm[:, 1:2], sm[:, 6:7]), "V")
                    vector.wait_ge(mk.sem["V"], v)
                    v = mk.inc(vector.tensor_mul(sm[:, 2:3], sm[:, 0:1],
                                                 sm[:, 1:2]), "V")
                    vector.wait_ge(mk.sem["V"], v)
                    recip_V[k] = mk.inc(vector.tensor_copy(
                        mk.colmeta[:, 8 * k:8 * k + 2], sm[:, 1:3]), "V")

                @block.tensor
                def _(tensor, i=i, tt=tt, k=k):
                    mk.wait(tensor, "V", recip_V[k])
                    if tt == 0 and i > 0:
                        mk.wait(tensor, "V", fix_V[i - 1])
                    tensor.transpose(
                        mk.pmini[0:1, 128 * tt:128 * tt + 128],
                        mk.smeta[k % 2][:, 1:2], mk.idenf[:])
                    mrT[k] = mk.inc(tensor.transpose(
                        mk.pmini2[0:1, 128 * tt:128 * tt + 128],
                        mk.smeta[k % 2][:, 2:3], mk.idenf[:]), "T")

            @block.tensor
            def _(tensor, i=i, b=b):
                if i >= 1:
                    mk.wait(tensor, "V", fix_V[i - 1])
                last = None
                for k in range(NF):
                    last = tensor.matmul(
                        mk.pp[0][0:E, 0:CH],
                        mk.wgs[:, E * k:E * k + E],
                        mk.xw[b][:, CH * k:CH * k + CH],
                        start=(k == 0), stop=(k == NF - 1))
                gate_T[i] = mk.inc(last, "T")
            xw_freeA = ("T", gate_T[i])

            @block.vector
            def _(vector, i=i):
                mk.wait(vector, "T", mrT[4 * i + 3])
                v = mk.inc(vector.tensor_copy(mk.rowch("r"),
                                              mk.pmini[0:1, 0:CH]), "V")
                mk.inc(vector.tensor_copy(mk.rowch("mr"),
                                          mk.pmini2[0:1, 0:CH]), "V")
            mr_V = mk.cnt["V"]

            @block.tensor
            def _(tensor, i=i, mr_V=mr_V):
                mk.wait(tensor, "V", mr_V)
                tensor.matmul(mk.pp[1][0:E, 0:CH], mk.onesrow[0:1, 0:E],
                              mk.rowch("mr"), start=True, stop=True)
                mk.inc(tensor.matmul(
                    mk.pmini[0:E, 0:CH], mk.onesrow[0:1, 0:E],
                    mk.rowch("r"), start=True, stop=True), "T")
            bc_T = mk.cnt["T"]

            @block.vector
            def _(vector, i=i, bc_T=bc_T):
                mk.wait(vector, "T", bc_T)
                c0 = CH * i
                lg = mk.lgT[:, c0:c0 + CH]
                v = mk.inc(vector.tensor_copy(lg, mk.pp[0][0:E, 0:CH]), "V")
                vector.wait_ge(mk.sem["V"], v)
                v = mk.inc(vector.scalar_tensor_tensor(
                    lg, mk.pp[1][0:E, 0:CH], mk.wgcs[0:E, 0:1], lg,
                    OP.mult, OP.subtract), "V")
                vector.wait_ge(mk.sem["V"], v)
                v = mk.inc(vector.tensor_mul(lg, lg, mk.pmini[0:E, 0:CH]),
                           "V")
                vector.wait_ge(mk.sem["V"], v)
                fix_V[i] = mk.inc(vector.tensor_scalar(
                    lg, lg, -1.0, None, OP.mult), "V")

            @block.sync
            def _(sync, i=i):
                if i + 2 < NCH:
                    s, v = xw_freeA
                    mk.wait(sync, s, v)
                    ld[i + 2] = load_chunk(sync, i + 2, mk.xw[i % 2])

        NT = T // 128
        am_T = {}
        am_psfree = {}
        exp_S_hist = {}
        bmT_hist = {}
        mt_V = {}
        for tt in range(NT):
            @block.tensor
            def _(tensor, tt=tt):
                if tt == 0:
                    mk.wait(tensor, "V", fix_V[NCH - 1])
                if tt - 2 >= 0:
                    s, v = am_psfree[tt - 2]
                    mk.wait(tensor, s, v)
                pdst = [mk.pmini, mk.pmini2][tt % 2]
                am_T[tt] = mk.inc(tensor.transpose(
                    pdst[0:128, 0:E], mk.lgT[:, 128 * tt:128 * tt + 128],
                    mk.idenf[0:E, 0:E]), "T")

            @block.vector
            def _(vector, tt=tt):
                mk.wait(vector, "T", am_T[tt])
                if tt > 0:
                    mk.wait(vector, "S", exp_S_hist[tt - 1])
                    mk.wait(vector, "T", bmT_hist[tt - 1])
                psrc = [mk.pmini, mk.pmini2][tt % 2]
                v = mk.inc(vector.tensor_copy(mk.tm_lg[:],
                                              psrc[0:128, 0:E]), "V")
                am_psfree[tt] = ("V", v)
                vector.wait_ge(mk.sem["V"], v)
                v = mk.inc(vector.reduce_max(mk.ameta[:, 0:1], mk.tm_lg[:],
                                             axis=AX), "V")
                vector.wait_ge(mk.sem["V"], v)
                vector.tensor_scalar(mk.tm_mx8[:], mk.tm_lg[:], 0.0,
                                     mk.ameta[:, 0:1], OP.mult, OP.add)
                v = mk.inc(vector.tensor_scalar(
                    mk.ameta[:, 1:2], mk.ameta[:, 0:1], -1.0, None, OP.mult),
                    "V")
                vector.wait_ge(mk.sem["V"], v)
                v = mk.inc(vector.max_index(mk.tm_idx[:], mk.tm_mx8[:],
                                            mk.tm_lg[:]), "V")
                vector.wait_ge(mk.sem["V"], v)
                mk.inc(vector.tensor_copy(mk.bmeta[:, 0:1],
                                          mk.tm_idx[:, 0:1]), "V")
            am_V = mk.cnt["V"]

            @block.scalar
            def _(scalar, tt=tt, am_V=am_V):
                mk.wait(scalar, "V", am_V)
                exp_S_hist[tt] = mk.inc(scalar.activation(
                    mk.tm_exp[:], mk.tm_lg[:], AF.Exp,
                    bias=mk.ameta[:, 1:2], scale=1.0,
                    accum_out=mk.ameta[:, 2:3]), "S")

            @block.vector
            def _(vector, tt=tt):
                mk.wait(vector, "S", exp_S_hist[tt])
                v = mk.inc(vector.reciprocal(mk.bmeta[:, 1:2],
                                             mk.ameta[:, 2:3]), "V")
                vector.wait_ge(mk.sem["V"], v)
                mk.inc(vector.tensor_copy(mk.col("gate", tt),
                                          mk.bmeta[:, 1:2]), "V")
            bm_V = mk.cnt["V"]

            @block.tensor
            def _(tensor, tt=tt, bm_V=bm_V):
                mk.wait(tensor, "V", bm_V)
                c = 128 * (tt % 4)
                bmT_hist[tt] = mk.inc(tensor.transpose(
                    mk.pq2[0][0:1, c:c + 128],
                    mk.bmeta[:, 0:1], mk.idenf[:]), "T")

            if tt % 4 == 3:
                i = tt // 4

                @block.vector
                def _(vector, i=i, tt=tt):
                    mk.wait(vector, "T", bmT_hist[tt])
                    mk.inc(vector.tensor_copy(mk.rowch("best"),
                                              mk.pq2[0][0:1, 0:CH]), "V")
                br_V = mk.cnt["V"]

                @block.tensor
                def _(tensor, i=i, br_V=br_V):
                    mk.wait(tensor, "V", br_V)
                    if i > 0:
                        mk.wait(tensor, "V", mt_V[i - 1])
                    mk.inc(tensor.matmul(
                        mk.pq2[1][0:E, 0:CH], mk.onesrow[0:1, 0:E],
                        mk.rowch("best"), start=True, stop=True), "T")
                bt_T = mk.cnt["T"]

                @block.vector
                def _(vector, i=i, bt_T=bt_T):
                    mk.wait(vector, "T", bt_T)
                    mt_V[i] = mk.inc(vector.tensor_scalar(
                        mk.mT[:, CH * i:CH * i + CH], mk.pq2[1][0:E, 0:CH],
                        mk.iotac[0:E, 0:1], None, OP.is_equal), "V")

        @block.vector
        def _(vector):
            mk.wait(vector, "V", mt_V[NCH - 1])
            for i in range(NCH):
                sl = slice(CH * i, CH * i + CH)
                init = 0.0 if i == 0 else mk.lgT[:, CH * i - 1:CH * i]
                v = mk.inc(vector.tensor_tensor_scan(
                    mk.lgT[:, sl], mk.mT[:, sl], mk.zerE[:], init,
                    OP.add, OP.add), "V")
                vector.wait_ge(mk.sem["V"], v)
            v = mk.inc(vector.tensor_copy(mk.cnt8[:], mk.lgT[:, T - 1:T]),
                       "V")
            vector.wait_ge(mk.sem["V"], v)
            mk.wait(vector, "G")
            v = mk.inc(vector.memset(mk.ccs[:], 0.0), "V")
            vector.wait_ge(mk.sem["V"], v)
            for arm in vector.switch_core_id(n=128):
                c = arm.logical % 8
                vector.tensor_copy(mk.ccs[0:E, c:c + 1], mk.cnt8[:])
            mk.inc(vector.engine_nop(), "V")
        ccs_V = mk.cnt["V"]

        @block.gpsimd
        def _(g):
            mk.wait(g, "V", ccs_V)
            mk.wait(g, "qG")
            mk.inc(g.dma_start(mk.cc_in[idx][:], mk.ccs[:]), "qG", 16)
            mk.wait(g, "qG")
            mk.inc(g.collective_compute(
                "AllReduce", OP.add,
                replica_groups=[list(range(NCORES))],
                ins=[mk.cc_in[idx].opt()], outs=[mk.cc_out[idx].opt()]), "C")
            mk.wait(g, "C")
            mk.inc(g.dma_start(mk.ccs[:], mk.cc_out[idx][:]), "qG", 16)
            mk.wait(g, "qG")
            mk.inc(g.engine_nop(), "G")
        coll_G = mk.cnt["G"]

        @block.vector
        def _(vector):
            mk.wait(vector, "G", coll_G)
            v = mk.inc(vector.tensor_mul(mk.cct[:], mk.ccs[0:E, 0:E],
                                         mk.selm[:]), "V")
            vector.wait_ge(mk.sem["V"], v)
            v = mk.inc(vector.reduce_sum(mk.off8[:], mk.cct[:], axis=AX), "V")
            vector.wait_ge(mk.sem["V"], v)
            v = mk.inc(vector.tensor_scalar(
                mk.lgT[:], mk.lgT[:], mk.off8[0:E, 0:1], None, OP.add), "V")
            vector.wait_ge(mk.sem["V"], v)
            v = mk.inc(vector.tensor_scalar(
                mk.lgT[:], mk.lgT[:], CAP + 0.5, None, OP.is_lt), "V")
            vector.wait_ge(mk.sem["V"], v)
            mk.inc(vector.tensor_mul(mk.lgT[:], mk.lgT[:], mk.mT[:]), "V")
        keepT_V = mk.cnt["V"]

        kc_V = {}
        for i in range(NCH):
            @block.tensor
            def _(tensor, i=i):
                mk.wait(tensor, "V", keepT_V if i == 0 else kc_V[i - 1])
                mk.inc(tensor.matmul(
                    mk.pmini[0:1, 0:CH], mk.onescol[0:E, 0:1],
                    mk.lgT[:, CH * i:CH * i + CH],
                    start=True, stop=True), "T")
            kr_T = mk.cnt["T"]

            @block.vector
            def _(vector, i=i, kr_T=kr_T):
                mk.wait(vector, "T", kr_T)
                mk.inc(vector.tensor_copy(mk.rowch("keep"),
                                          mk.pmini[0:1, 0:CH]), "V")
            krow_V = mk.cnt["V"]

            @block.tensor
            def _(tensor, i=i, krow_V=krow_V):
                mk.wait(tensor, "V", krow_V)
                last = None
                for tt in range(4):
                    last = tensor.transpose(
                        mk.pmini2[0:128, tt:tt + 1],
                        mk.rch[0:1, CH * mk.RCH["keep"] + 128 * tt:
                               CH * mk.RCH["keep"] + 128 * tt + 128],
                        mk.idenf[0:1, 0:1])
                mk.inc(last, "T")
            kcT = mk.cnt["T"]

            @block.vector
            def _(vector, i=i, kcT=kcT):
                mk.wait(vector, "T", kcT)
                last = None
                for tt in range(4):
                    last = vector.tensor_copy(
                        mk.col("keep", 4 * i + tt), mk.pmini2[0:128, tt:tt + 1])
                kc_V[i] = mk.inc(last, "V")

        @block.vector
        def _(vector):
            mk.wait(vector, "V", kc_V[NCH - 1])
            v = mk.inc(vector.scalar_tensor_tensor(
                mk.colslice("geff"), mk.colslice("gate"), 0.5,
                mk.colslice("keep"), OP.mult, OP.mult), "V")
            vector.wait_ge(mk.sem["V"], v)
            v = mk.inc(vector.tensor_mul(
                mk.colslice("ga"), mk.colslice("r"), mk.colslice("geff")),
                "V")
            vector.wait_ge(mk.sem["V"], v)
            mk.inc(vector.tensor_mul(
                mk.colslice("gb"), mk.colslice("mr"), mk.colslice("geff")),
                "V")
        grow_V = mk.cnt["V"]

        ld2 = {}
        rowsB_V = {}
        ab_T = {}
        ln_V = {}
        xw_st = {}
        exp_T = {}
        pp_free = {}

        @block.sync
        def _(sync):
            mk.wait(sync, "T", gate_T[NCH - 1])
            ld2[0] = load_chunk(sync, 0, mk.xw[0])
            ld2[1] = load_chunk(sync, 1, mk.xw[1])

        for i in range(NCH):
            b = i % 2

            @block.tensor
            def _(tensor, i=i):
                mk.wait(tensor, "V", grow_V)
                if i >= 1:
                    mk.wait(tensor, "V", rowsB_V[i - 1])
                    s, v = pp_free[i * NF - 1]
                    mk.wait(tensor, s, v)
                last = None
                for tt in range(4):
                    k = 4 * i + tt
                    c = 128 * tt
                    tensor.transpose(mk.pmini[0:1, c:c + 128],
                                     mk.col("ga", k), mk.idenf[:])
                    tensor.transpose(mk.pmini2[0:1, c:c + 128],
                                     mk.col("gb", k), mk.idenf[:])
                    last = tensor.transpose(mk.pp[1][0:1, c:c + 128],
                                            mk.col("geff", k), mk.idenf[:])
                mk.inc(last, "T")
            rB_T = mk.cnt["T"]

            @block.vector
            def _(vector, i=i, rB_T=rB_T):
                mk.wait(vector, "T", rB_T)
                vector.tensor_copy(mk.rowch("ga"), mk.pmini[0:1, 0:CH])
                vector.tensor_copy(mk.rowch("gb"), mk.pmini2[0:1, 0:CH])
                rowsB_V[i] = mk.inc(vector.tensor_copy(
                    mk.geffb_ch[:], mk.pp[1][0:1, 0:CH]), "V")

            @block.tensor
            def _(tensor, i=i):
                mk.wait(tensor, "V", rowsB_V[i])
                if i >= 1:
                    mk.wait(tensor, "V", ln_V[i - 1])
                tensor.matmul(mk.pq2[0][:, 0:CH], mk.onesrow[0:1, :],
                              mk.rowch("ga"), start=True, stop=True)
                ab_T[i] = mk.inc(tensor.matmul(
                    mk.pq2[1][:, 0:CH], mk.onesrow[0:1, :],
                    mk.rowch("gb"), start=True, stop=True), "T")

            @block.vector
            def _(vector, i=i, b=b):
                mk.wait(vector, *ld2[i])
                mk.wait(vector, "T", ab_T[i])
                if i >= 1:
                    mk.wait(vector, "T", exp_T[i * NF - 1])
                for f in range(NF):
                    sl = slice(CH * f, CH * f + CH)
                    vector.tensor_mul(mk.lnt[:, sl], mk.xw[b][:, sl],
                                      mk.pq2[0][:, 0:CH])
                v = mk.inc(vector.engine_nop(), "V")
                vector.wait_ge(mk.sem["V"], v)
                last = None
                for f in range(NF):
                    sl = slice(CH * f, CH * f + CH)
                    last = vector.tensor_tensor(
                        mk.lnw[:, sl], mk.lnt[:, sl], mk.pq2[1][:, 0:CH],
                        OP.subtract)
                ln_V[i] = mk.inc(last, "V")

            for mo2 in range(NF // 2):
                @block.tensor
                def _(tensor, i=i, mo2=mo2):
                    mk.wait(tensor, "V", ln_V[i])
                    if i == 0 and mo2 == 0:
                        mk.wait(tensor, "qG", wes_qG)
                    for mo in (2 * mo2, 2 * mo2 + 1):
                        gi = i * NF + mo
                        if gi - 2 >= 0 and (gi - 2) in pp_free:
                            s, v = pp_free[gi - 2]
                            mk.wait(tensor, s, v)
                        p = mk.pp[mo % 2]
                        for k in range(NF):
                            tensor.matmul(
                                p[:], mk.wes[:, D * k + 128 * mo:
                                             D * k + 128 * mo + 128],
                                mk.lnw[:, CH * k:CH * k + CH],
                                start=(k == 0), stop=False)
                        exp_T[gi] = mk.inc(tensor.matmul(
                            p[:], mk.bexps[0:1, 128 * mo:128 * mo + 128],
                            mk.geffb_ch[:], start=False, stop=True), "T")

                @block.vector
                def _(vector, i=i, b=b, mo2=mo2):
                    for mo in (2 * mo2, 2 * mo2 + 1):
                        gi = i * NF + mo
                        mk.wait(vector, "T", exp_T[gi])
                        sl = slice(CH * mo, CH * mo + CH)
                        v = mk.inc(vector.tensor_add(
                            mk.xw[b][:, sl], mk.xw[b][:, sl],
                            mk.pp[mo % 2][:]), "V")
                        pp_free[gi] = ("V", v)
            ev_V = mk.cnt["V"]

            @block.sync
            def _(sync, i=i, b=b, ev_V=ev_V):
                mk.wait(sync, "V", ev_V)
                xw_st[i] = store_chunk(sync, i, mk.xw[b])
                if i + 2 < NCH:
                    mk.wait(sync, *xw_st[i])
                    ld2[i + 2] = load_chunk(sync, i + 2, mk.xw[b])

        return [("V", mk.cnt["V"]), ("qSt", mk.cnt["qSt"]),
                ("T", mk.cnt["T"])]

    def p_attn(self, block, b, pre_wait=()):
        mk = self
        aF = self.arenaF
        aFB = self.aFB
        aB = self.arenaB

        QF, KF, VT = 0, 2048, 4096
        EP, E2O, ESO = 6208, 15424, 17984
        A2O, PTO, ATO, WSO = 20544, 25152, 25664, 45056
        PADL = 128
        EPW, E2W, ESW, A2W = 2304, 1280, 1280, 1152

        def bufB(off, w, p0=0, pn=128):
            return aFB[p0:p0 + pn, off:off + w]

        A = dict(
            Qfm=bufB(QF, 2048), Kfm=bufB(KF, 2048), Vtm=bufB(VT, 2112),
            E1P=[bufB(EP + i * EPW, EPW) for i in range(4)],
            E2b=[bufB(E2O + h * E2W, E2W) for h in range(2)],
            E2sb=[bufB(ESO + h * ESW, ESW) for h in range(2)],
            A2b=[bufB(A2O + i * A2W, A2W) for i in range(4)],
            PTb=[bufB(PTO + h * 256, 256) for h in range(2)],
            Vfmt=bufB(PTO, 512),
            A2Tb=[bufB(ATO + h * 256, 256) for h in range(2)],
            negb=[aF[:, 16384 + 16 * h:16400 + 16 * h] for h in range(2)],
            bkbp=[aF[:, 16416 + h:16417 + h] for h in range(2)],
            tmpln=aF[:, 16432:16944],
            rcbc=aF[:, 16944:17456],
            qcol=[aF[:, 17456 + 16 * h:17472 + 16 * h] for h in range(2)],
            e2cf=[aF[:, 17488 + h:17489 + h] for h in range(2)],
            nrow=aF[0:1, 17504:21600],
            rcrow=aF[0:1, 21600:22112],
            bks=aF[0:1, 22112:22144],
            ln2b=aB[:, 0:16384], aob=aB[:, 16384:32768],
            pm_bf=self.pmini[:, 0:256].bitcast(BF16),
            pm2_bf=self.pmini2[:, 0:256].bitcast(BF16),
            vt_bf=self.pq21[:, 512:768].bitcast(BF16),
            PADL=PADL, EP=EP, EPW=EPW, E2O=E2O, ESO=ESO, A2O=A2O, WSO=WSO,
            aFB=aFB, aF=aF,
        )
        A["FBH"] = A["Qfm"].tensor
        A["FBP"] = 2 * 28672

        xdb = [self.xd[128 * f:128 * f + 128, S * b:S * b + S]
               for f in range(NF)]

        bks = A["bks"]
        if b == 0:
            @block.scalar
            def _(scalar):
                mk.wait(scalar, "qG")
                for s, v in pre_wait:
                    mk.wait(scalar, s, v)
                mk.inc(scalar.activation(
                    bufB(EP, 1152, 0, 64), mk.pekTs[0:64, :], AF.Square),
                    "S")
            pk_S = mk.cnt["S"]

            @block.tensor
            def _(tensor):
                mk.wait(tensor, "S", pk_S)
                for s, v in pre_wait:
                    mk.wait(tensor, s, v)
                tensor.matmul(self.pq2[0][0:1, 0:512], mk.onesb[0:64, 0:1],
                              bufB(EP, 512, 0, 64), start=True, stop=True)
                tensor.matmul(self.pq2[0][0:1, 512:1024],
                              mk.onesb[0:64, 0:1],
                              bufB(EP + 512, 512, 0, 64),
                              start=True, stop=True)
                mk.inc(tensor.matmul(
                    self.pmini[0:1, 0:128], mk.onesb[0:64, 0:1],
                    bufB(EP + 1024, 128, 0, 64), start=True, stop=True), "T")
            pk_T = mk.cnt["T"]

            @block.vector
            def _(vector):
                mk.wait(vector, "T", pk_T)
                vector.reduce_max(bks[0:1, 2:3], self.pq2[0][0:1, 0:512],
                                  axis=AX)
                vector.reduce_max(bks[0:1, 3:4], self.pq2[0][0:1, 512:1024],
                                  axis=AX)
                v = mk.inc(vector.reduce_max(
                    bks[0:1, 4:5], self.pmini[0:1, 0:128], axis=AX), "V")
                vector.wait_ge(mk.sem["V"], v)
                mk.inc(vector.reduce_max(bks[0:1, 0:1], bks[0:1, 2:5],
                                         axis=AX), "V")
            bp_V = mk.cnt["V"]

            @block.scalar
            def _(scalar):
                mk.wait(scalar, "V", bp_V)
                mk.inc(scalar.activation(bks[0:1, 1:2], bks[0:1, 0:1],
                                         AF.Sqrt), "S")
            self.bp_S = mk.cnt["S"]

        @block.sync
        def _(sync):
            for s, v in pre_wait:
                mk.wait(sync, s, v)
            if b == 0:
                mk.wait(sync, "T")
            if mk.cnt["qL0"] > 0:
                mk.wait(sync, "qL0")
            for f in range(NF):
                mk.inc(sync.dma_start(aF[:, 2048 * f:2048 * f + 2048],
                                      xdb[f]), "qL0", 16)
        xb_q = mk.cnt["qL0"]

        st_T = {}
        bnB_V = {}
        mrB_T = {}
        lnB_V = {}
        for ci in range(4):
            for tt in range(4):
                k = 4 * ci + tt

                @block.tensor
                def _(tensor, ci=ci, tt=tt, k=k):
                    mk.wait(tensor, "qL0", xb_q)
                    if k == 0:
                        for s, v in pre_wait:
                            mk.wait(tensor, s, v)
                        mk.wait(tensor, "V")
                    if k - 2 >= 0:
                        mk.wait(tensor, "V", bnB_V[k - 2])
                    last = None
                    for f in range(NF):
                        last = tensor.transpose(
                            mk.pq2[k % 2][:, 128 * f:128 * f + 128],
                            aF[:, 2048 * f + 128 * k:
                               2048 * f + 128 * k + 128],
                            mk.idenf[:])
                    st_T[k] = mk.inc(last, "T")

                @block.vector
                def _(vector, k=k):
                    mk.wait(vector, "T", st_T[k])
                    p = mk.pq2[k % 2]
                    sm = mk.smeta[k % 2]
                    vector.bn_stats(mk.bn6[:, 0:6], p[:, 0:512])
                    v = mk.inc(vector.bn_stats(mk.bn6[:, 6:12],
                                               p[:, 512:1024]), "V")
                    bnB_V[k] = v
                    vector.wait_ge(mk.sem["V"], v)
                    v = mk.inc(vector.bn_aggr(mk.mvar[:], mk.bn6[:, 0:12]),
                               "V")
                    vector.wait_ge(mk.sem["V"], v)
                    if k - 2 >= 0:
                        mk.wait(vector, "T", mrB_T[k - 2])
                    vector.tensor_copy(sm[:, 0:1], mk.mvar[:, 0:1])
                    mk.inc(vector.tensor_scalar(
                        sm[:, 5:6], mk.mvar[:, 1:2], EPS, None, OP.add), "V")
                sq_in = mk.cnt["V"]

                @block.scalar
                def _(scalar, k=k, sq_in=sq_in):
                    mk.wait(scalar, "V", sq_in)
                    mk.inc(scalar.activation(
                        mk.smeta[k % 2][:, 6:7], mk.smeta[k % 2][:, 5:6],
                        AF.Sqrt), "S")
                sq_out = mk.cnt["S"]

                @block.vector
                def _(vector, k=k, sq_out=sq_out):
                    mk.wait(vector, "S", sq_out)
                    sm = mk.smeta[k % 2]
                    v = mk.inc(vector.reciprocal(sm[:, 1:2], sm[:, 6:7]),
                               "V")
                    vector.wait_ge(mk.sem["V"], v)
                    mk.inc(vector.tensor_mul(sm[:, 2:3], sm[:, 0:1],
                                             sm[:, 1:2]), "V")
                mr_in = mk.cnt["V"]

                @block.tensor
                def _(tensor, ci=ci, tt=tt, k=k, mr_in=mr_in):
                    mk.wait(tensor, "V", mr_in)
                    if tt == 0 and ci > 0:
                        mk.wait(tensor, "V", lnB_V[ci - 1])
                    tensor.transpose(
                        mk.pmini[0:1, 128 * tt:128 * tt + 128],
                        mk.smeta[k % 2][:, 1:2], mk.idenf[:])
                    mrB_T[k] = mk.inc(tensor.transpose(
                        mk.pmini2[0:1, 128 * tt:128 * tt + 128],
                        mk.smeta[k % 2][:, 2:3], mk.idenf[:]), "T")

            @block.vector
            def _(vector, ci=ci):
                mk.wait(vector, "T", mrB_T[4 * ci + 3])
                vector.tensor_copy(mk.rowch("r"), mk.pmini[0:1, 0:CH])
                mk.inc(vector.tensor_copy(
                    mk.rowch("mr"), mk.pmini2[0:1, 0:CH]), "V")
            rowB_V = mk.cnt["V"]

            @block.tensor
            def _(tensor, ci=ci, rowB_V=rowB_V):
                mk.wait(tensor, "V", rowB_V)
                tensor.matmul(mk.pp[0][:, 0:CH], mk.onesrow[0:1, :],
                              mk.rowch("r"), start=True, stop=True)
                mk.inc(tensor.matmul(
                    mk.pp[1][:, 0:CH], mk.onesrow[0:1, :],
                    mk.rowch("mr"), start=True, stop=True), "T")
            bcB_T = mk.cnt["T"]

            @block.vector
            def _(vector, ci=ci, bcB_T=bcB_T):
                mk.wait(vector, "T", bcB_T)
                for f in range(NF):
                    xsl = aF[:, 2048 * f + 512 * ci:
                             2048 * f + 512 * ci + 512]
                    lsl = A["ln2b"][:, 2048 * f + 512 * ci:
                                    2048 * f + 512 * ci + 512]
                    v = mk.inc(vector.tensor_mul(A["tmpln"][:], xsl,
                                                 mk.pp[0][:, 0:CH]), "V")
                    vector.wait_ge(mk.sem["V"], v)
                    v = mk.inc(vector.tensor_tensor(
                        lsl, A["tmpln"][:], mk.pp[1][:, 0:CH], OP.subtract),
                        "V")
                    vector.wait_ge(mk.sem["V"], v)
                lnB_V[ci] = mk.cnt["V"]
        A["ln2_V"] = lnB_V[3]

        @block.vector
        def _(vector):
            mk.wait(vector, "V", A["ln2_V"])
            for i in range(4):
                vector.memset(A["E1P"][i][:, :], 0.0)
                vector.memset(A["A2b"][i][:, :], 0.0)
            vector.memset(A["E2b"][0][:, :], 0.0)
            vector.memset(A["E2b"][1][:, :], 0.0)
            vector.memset(A["E2sb"][0][:, :], 0.0)
            vector.memset(A["E2sb"][1][:, :], 0.0)
            for kt in range(16):
                vector.memset(A["Vtm"][:, 132 * kt + 64:132 * kt + 65], 1.0)
                vector.memset(A["Vtm"][:, 132 * kt + 130:132 * kt + 131],
                              1.0)
            mk.inc(vector.engine_nop(), "V")
        A["pads_V"] = mk.cnt["V"]

        A["wsl_q"] = {}

        def wslice_load(g, p):
            if mk.cnt["qG"] > 0:
                mk.wait(g, "qG")
            base = WSO + (p % 2) * 3072
            last = None
            for wi, wt in enumerate((self.wqT, self.wkT, self.wvT)):
                for k in range(NF):
                    dstc = base + (wi * NF + k) * 128
                    last = mk.inc(g.dma_start(
                        aFB[:, dstc:dstc + 128],
                        wt[128 * k:128 * k + 128, 128 * p:128 * p + 128]),
                        "qG", 16)
            A["wsl_q"][p] = last
        A["wslice_load"] = wslice_load

        @block.gpsimd
        def _(g):
            for s, v in pre_wait:
                mk.wait(g, s, v)
            wslice_load(g, 0)

        A["prev_tail"] = ("V", A["pads_V"])
        A["prev_qkv_T"] = mk.cnt["T"]
        for p in range(H // 2):
            self._emit_pair(block, b, p, A)

        self._emit_wo(block, b, A)

        return [("V", mk.cnt["V"]), ("T", mk.cnt["T"]),
                ("qSt", mk.cnt["qSt"]), ("S", mk.cnt["S"]),
                ("qG", mk.cnt["qG"]), ("qA", mk.cnt["qA"]),
                ("qL0", mk.cnt["qL0"]), ("qS", mk.cnt["qS"])]

    def _emit_pair(self, block, b, p, A):
        mk = self
        Qfm, Kfm, Vtm = A["Qfm"], A["Kfm"], A["Vtm"]
        E1P, E2b, E2sb, A2b = A["E1P"], A["E2b"], A["E2sb"], A["A2b"]
        PTb, Vfmt, A2Tb = A["PTb"], A["Vfmt"], A["A2Tb"]
        negb, bkbp, qcol = A["negb"], A["bkbp"], A["qcol"]
        nrow, rcrow, bks, rcbc = A["nrow"], A["rcrow"], A["bks"], A["rcbc"]
        ln2b, aob = A["ln2b"], A["aob"]
        pm_bf, pm2_bf, vt_bf = A["pm_bf"], A["pm2_bf"], A["vt_bf"]
        PADL, EP, EPW = A["PADL"], A["EP"], A["EPW"]
        WSO, FBH, FBP = A["WSO"], A["FBH"], A["FBP"]
        aFB = A["aFB"]

        Qp = self.pq2[0][:, 0:512]
        Kp = self.pq2[0][:, 512:1024]
        Vp = self.pq2[1][:, 0:512]
        wbase = WSO + (p % 2) * 3072

        def wsl(wi, k):
            c = wbase + (wi * NF + k) * 128
            return aFB[:, c:c + 128]

        ev_V = {}
        for nc4 in range(4):
            @block.tensor
            def _(tensor, nc4=nc4):
                mk.wait(tensor, "qG", A["wsl_q"][p])
                mk.wait(tensor, "V", A["ln2_V"])
                if nc4 == 0:
                    s, v = A["prev_tail"]
                    mk.wait(tensor, s, v)
                else:
                    mk.wait(tensor, "V", ev_V[nc4 - 1])
                last = None
                for k in range(NF):
                    mv = ln2b[:, 2048 * k + 512 * nc4:
                              2048 * k + 512 * nc4 + 512]
                    tensor.matmul(Qp, wsl(0, k), mv,
                                  start=(k == 0), stop=(k == NF - 1))
                    tensor.matmul(Kp, wsl(1, k), mv,
                                  start=(k == 0), stop=(k == NF - 1))
                    last = tensor.matmul(Vp, wsl(2, k), mv,
                                         start=(k == 0), stop=(k == NF - 1))
                mk.inc(last, "T")
            qkv_T = mk.cnt["T"]

            @block.vector
            def _(vector, nc4=nc4, qkv_T=qkv_T):
                mk.wait(vector, "T", qkv_T)
                tsl = slice(512 * nc4, 512 * nc4 + 512)
                vector.tensor_copy(Qfm[:, tsl], Qp)
                vector.tensor_copy(Kfm[:, tsl], Kp)
                mk.inc(vector.tensor_copy(Vfmt[:], Vp), "V")
            vf_V = mk.cnt["V"]

            @block.tensor
            def _(tensor, nc4=nc4, vf_V=vf_V):
                mk.wait(tensor, "V", vf_V)
                last = None
                for j in range(4):
                    last = tensor.transpose(
                        vt_bf[:, 128 * j:128 * j + 128],
                        Vfmt[:, 128 * j:128 * j + 128], mk.idenb[:])
                mk.inc(last, "T")
            vtr_T = mk.cnt["T"]

            @block.vector
            def _(vector, nc4=nc4, vtr_T=vtr_T):
                mk.wait(vector, "T", vtr_T)
                last = None
                for j in range(4):
                    kt = 4 * nc4 + j
                    vector.tensor_tensor(
                        Vtm[:, 132 * kt:132 * kt + 64],
                        vt_bf[:, 128 * j:128 * j + 64],
                        mk.pev0bc[:], OP.add)
                    last = vector.tensor_tensor(
                        Vtm[:, 132 * kt + 66:132 * kt + 130],
                        vt_bf[:, 128 * j + 64:128 * j + 128],
                        mk.pev0bc[:], OP.add)
                ev_V[nc4] = mk.inc(last, "V")
        qkv_done_V = ev_V[3]
        qkv_done_T = mk.cnt["T"]

        if p + 1 < H // 2:
            @block.gpsimd
            def _(g):
                mk.wait(g, "T", qkv_done_T)
                A["wslice_load"](g, p + 1)

        for which, srcfm in (("q", Qfm), ("k", Kfm)):
            @block.scalar
            def _(scalar, which=which, srcfm=srcfm):
                mk.wait(scalar, "V", qkv_done_V)
                if which == "q":
                    s, v = A["prev_tail"]
                    mk.wait(scalar, s, v)
                else:
                    mk.wait(scalar, "V")
                mk.inc(scalar.activation(
                    E1P[0][:, 256:2304], srcfm[:], AF.Square), "S")
            sq_S = mk.cnt["S"]

            nrm_T = {}
            for nc4 in range(4):
                @block.tensor
                def _(tensor, nc4=nc4, sq_S=sq_S):
                    mk.wait(tensor, "S", sq_S)
                    if nc4 >= 1:
                        mk.wait(tensor, "V")
                    tsl = slice(256 + 512 * nc4, 256 + 512 * nc4 + 512)
                    tensor.matmul(self.pmini[0:1, 0:512],
                                  mk.onesb[0:64, 0:1],
                                  E1P[0][0:64, tsl], start=True, stop=True,
                                  tile_position=(0, 0))
                    nrm_T[nc4] = mk.inc(tensor.matmul(
                        self.pmini2[0:1, 0:512], mk.onesb[64:128, 0:1],
                        E1P[0][64:128, tsl], start=True, stop=True,
                        tile_position=(64, 0)), "T")

                @block.vector
                def _(vector, nc4=nc4):
                    mk.wait(vector, "T", nrm_T[nc4])
                    vector.tensor_copy(nrow[0:1, 512 * nc4:512 * nc4 + 512],
                                       self.pmini[0:1, 0:512])
                    mk.inc(vector.tensor_copy(
                        nrow[0:1, 2048 + 512 * nc4:2048 + 512 * nc4 + 512],
                        self.pmini2[0:1, 0:512]), "V")
            nrows_V = mk.cnt["V"]

            if which == "q":
                qc_T = {}
                for h in range(2):
                    @block.tensor
                    def _(tensor, h=h, nrows_V=nrows_V):
                        mk.wait(tensor, "V", nrows_V)
                        last = None
                        for s in range(16):
                            o = 2048 * h + 128 * s
                            last = tensor.transpose(
                                [self.pmini, self.pmini2][h][0:128,
                                                             256 + s:257 + s],
                                nrow[0:1, o:o + 128], mk.idenf[0:1, 0:1])
                        qc_T[h] = mk.inc(last, "T")

                    @block.vector
                    def _(vector, h=h):
                        mk.wait(vector, "T", qc_T[h])
                        mk.inc(vector.tensor_copy(
                            qcol[h][:],
                            [self.pmini, self.pmini2][h][0:128, 256:272]),
                            "V")
            else:
                @block.vector
                def _(vector, nrows_V=nrows_V):
                    mk.wait(vector, "V", nrows_V)
                    vector.reduce_max(bks[0:1, 8:9], nrow[0:1, 0:2048],
                                      axis=AX)
                    mk.inc(vector.reduce_max(
                        bks[0:1, 9:10], nrow[0:1, 2048:4096], axis=AX), "V")
                bkm_V = mk.cnt["V"]

                @block.scalar
                def _(scalar, bkm_V=bkm_V):
                    mk.wait(scalar, "V", bkm_V)
                    if b == 0 and p == 0:
                        mk.wait(scalar, "S", self.bp_S)
                    scalar.activation(bks[0:1, 10:11], bks[0:1, 8:9],
                                      AF.Sqrt)
                    mk.inc(scalar.activation(bks[0:1, 11:12],
                                             bks[0:1, 9:10], AF.Sqrt), "S")
                bky_S = mk.cnt["S"]

                @block.vector
                def _(vector, bky_S=bky_S):
                    mk.wait(vector, "S", bky_S)
                    vector.tensor_tensor(bks[0:1, 12:13], bks[0:1, 10:11],
                                         bks[0:1, 1:2], OP.add)
                    mk.inc(vector.tensor_tensor(
                        bks[0:1, 13:14], bks[0:1, 11:12], bks[0:1, 1:2],
                        OP.add), "V")
                bka_V = mk.cnt["V"]

                @block.tensor
                def _(tensor, bka_V=bka_V):
                    mk.wait(tensor, "V", bka_V)
                    tensor.matmul(self.pmini[0:128, 272:273],
                                  mk.onesrow[0:1, :], bks[0:1, 12:13],
                                  start=True, stop=True)
                    mk.inc(tensor.matmul(
                        self.pmini2[0:128, 272:273], mk.onesrow[0:1, :],
                        bks[0:1, 13:14], start=True, stop=True), "T")
                bkb_T = mk.cnt["T"]

                @block.vector
                def _(vector, bkb_T=bkb_T):
                    mk.wait(vector, "T", bkb_T)
                    vector.tensor_copy(bkbp[0][:],
                                       self.pmini[0:128, 272:273])
                    mk.inc(vector.tensor_copy(
                        bkbp[1][:], self.pmini2[0:128, 272:273]), "V")

        @block.scalar
        def _(scalar):
            mk.wait(scalar, "V")
            scalar.activation(qcol[0][:], qcol[0][:], AF.Sqrt)
            mk.inc(scalar.activation(qcol[1][:], qcol[1][:], AF.Sqrt), "S")
        nq_S = mk.cnt["S"]

        @block.vector
        def _(vector, nq_S=nq_S):
            mk.wait(vector, "S", nq_S)
            vector.tensor_scalar(negb[0][:], qcol[0][:], -0.125,
                                 bkbp[0][:, 0:1], OP.mult, OP.mult)
            mk.inc(vector.tensor_scalar(
                negb[1][:], qcol[1][:], -0.125,
                bkbp[1][:, 0:1], OP.mult, OP.mult), "V")
        prep_V = mk.cnt["V"]

        subst = dict(prep_V=prep_V, e1_S={}, pmul_V={}, shearA_qA={},
                     e2_S={}, last_scoreS={}, tail_V=A["pads_V"],
                     flip_ev=A.get("last_flip_ev", A["pads_V"]),
                     shear_q={-1: A.get("last_shear_q", 0)},
                     avstart=None)
        for sb in range(8):
            for s01 in range(2):
                self._emit_sub(block, b, p, 2 * sb + s01, A, subst)
            self._emit_tail(block, b, p, sb, A, subst)
        A["last_shear_q"] = subst["shear_q"][15]
        A["last_flip_ev"] = subst["flip_ev"]
        A["prev_tail"] = ("V", subst["tail_V"])

    def _emit_sub(self, block, b, p, qi, A, SS):
        mk = self
        PADL, EPW = A["PADL"], A["EPW"]
        FBH, FBP = A["FBH"], A["FBP"]
        q0 = 128 * qi
        si = qi % 2
        kfar = q0 - 1024 if q0 >= 1024 else 0
        NZW = q0 + 128 - kfar
        C0 = kfar - q0 + 1024
        jexp0 = max(0, 897 - q0)
        jc0 = (jexp0 // 512) * 512
        jA0 = max(0, ((897 - q0) // 128) * 128) if q0 < 897 else 0
        q1 = 128 * (qi + 1) if qi % 2 == 0 else None
        jA0n = (max(0, ((897 - q0 - 128) // 128) * 128)
                if q0 + 128 < 897 else 0) if qi % 2 == 0 else None

        Qfm, Kfm = A["Qfm"], A["Kfm"]
        E1P, E2b, E2sb, A2b = A["E1P"], A["E2b"], A["E2sb"], A["A2b"]
        negb = A["negb"]
        e2cf = A["e2cf"]
        ebuf = [E1P[2 * si + h] for h in range(2)]
        EPoff = [A["EP"] + (2 * si + h) * EPW for h in range(2)]
        a2buf = [A2b[2 * si + h] for h in range(2)]
        A2off = [A["A2O"] + (2 * si + h) * 1152 for h in range(2)]

        @block.tensor
        def _(tensor):
            mk.wait(tensor, "V", SS["prep_V"])
            mk.wait(tensor, "V", SS["flip_ev"])
            if qi >= 1:
                mk.wait(tensor, "S", SS["last_scoreS"][qi - 1])
            if qi >= 2:
                mk.wait(tensor, "V", SS["pmul_V"][qi - 2])
            last = None
            for jc in range(jc0, 1152, 512):
                w = min(512, 1152 - jc)
                for h in range(2):
                    dst = (self.pq2[h][:, jc:jc + w] if jc < 1024 else
                           self.pp[h][:, 256:256 + w])
                    last = tensor.matmul(
                        dst, Qfm[64 * h:64 * h + 64, q0:q0 + 128],
                        mk.pekTs[64 * h:64 * h + 64, jc:jc + w],
                        start=True, stop=True,
                        tile_position=(64 * h, 0),
                        skip_group_check=True)
            mk.inc(last, "T")
        pq_T = mk.cnt["T"]

        @block.scalar
        def _(scalar):
            mk.wait(scalar, "T", pq_T)
            if SS["shear_q"].get(qi - 1):
                mk.wait(scalar, "qS", SS["shear_q"][qi - 1])
            last = None
            for h in range(2):
                if jexp0 < 1024:
                    last = scalar.activation(
                        E2b[h][:, PADL + jexp0:PADL + 1024],
                        self.pq2[h][:, jexp0:1024], AF.Exp,
                        bias=negb[h][:, qi:qi + 1], scale=0.125)
                last = scalar.activation(
                    E2b[h][:, PADL + 1024:PADL + 1025],
                    self.pp[h][:, 256:257], AF.Exp,
                    bias=negb[h][:, qi:qi + 1], scale=0.125)
            mk.inc(last, "S")
        SS["e2_S"][qi] = mk.cnt["S"]

        @block.vector
        def _(vector):
            mk.wait(vector, "S", SS["e2_S"][qi])
            if q0 >= 1024 and SS["shear_q"].get(qi - 1):
                mk.wait(vector, "qS", SS["shear_q"][qi - 1])
            if q0 >= 1024 and SS["pmul_V"].get(qi - 1):
                mk.wait(vector, "V", SS["pmul_V"][qi - 1])
            if q0 >= 1024:
                for h in range(2):
                    v = mk.inc(vector.tensor_copy(
                        e2cf[h][:], E2b[h][:, PADL:PADL + 1]), "V")
                    vector.wait_ge(mk.sem["V"], v)
                    vector.tensor_scalar(
                        E2b[h][:, 0:PADL], mk.idenf[:], 0.0,
                        e2cf[h][:, 0:1], OP.mult, OP.add)
            mk.inc(vector.engine_nop(), "V")
        fill_V = mk.cnt["V"]

        @block.sync
        def _(sync):
            mk.wait(sync, "V", fill_V)
            if mk.cnt["qS"] > 0:
                mk.wait(sync, "qS")
            if mk.cnt["qA"] > 0:
                mk.wait(sync, "qA")
            for h in range(2):
                srcoff = A["E2O"] + h * 1280 + PADL + C0
                ap = bass.AP(FBH, srcoff, [[FBP - 1, 128], [1, NZW]])
                mk.inc(sync.dma_start(E2sb[h][:, 128:128 + NZW], ap),
                       "qS", 16)
        shear_q = mk.cnt["qS"]
        SS["shear_q"][qi] = shear_q

        prevS = None
        for ks in range(0, q0 + 128, 1024):
            wsup = min(1024, q0 + 128 - ks)

            @block.tensor
            def _(tensor, ks=ks, wsup=wsup, prevS=prevS):
                if ks == 0:
                    mk.wait(tensor, "S", SS["e2_S"][qi])
                else:
                    mk.wait(tensor, "S", prevS)
                last = None
                for kc in range(ks, ks + wsup, 512):
                    w = min(512, q0 + 128 - kc)
                    for h in range(2):
                        last = tensor.matmul(
                            self.pq2[h][:, kc - ks:kc - ks + w],
                            Qfm[64 * h:64 * h + 64, q0:q0 + 128],
                            Kfm[64 * h:64 * h + 64, kc:kc + w],
                            start=True, stop=True,
                            tile_position=(64 * h, 0))
                mk.inc(last, "T")
            sc_T = mk.cnt["T"]

            @block.scalar
            def _(scalar, ks=ks, wsup=wsup, sc_T=sc_T):
                mk.wait(scalar, "T", sc_T)
                last = None
                for h in range(2):
                    last = scalar.activation(
                        ebuf[h][:, 256 + ks:256 + ks + wsup],
                        self.pq2[h][:, 0:wsup], AF.Exp, scale=0.125)
                mk.inc(last, "S")
            prevS = mk.cnt["S"]
        SS["last_scoreS"][qi] = prevS

        @block.vector
        def _(vector):
            mk.wait(vector, "S", SS["last_scoreS"][qi])
            mk.wait(vector, "qS", shear_q)
            last = None
            for h in range(2):
                if kfar > 0:
                    vector.tensor_scalar(
                        ebuf[h][:, 256:256 + kfar],
                        ebuf[h][:, 256:256 + kfar],
                        e2cf[h][:, 0:1], None, OP.mult)
                last = vector.tensor_mul(
                    ebuf[h][:, 256 + kfar:256 + q0 + 128],
                    ebuf[h][:, 256 + kfar:256 + q0 + 128],
                    E2sb[h][:, 128:128 + NZW])
                if qi % 2 == 0:
                    last = vector.memset(
                        ebuf[h][:, 256 + q0 + 128:256 + q0 + 256], 0.0)
            mk.inc(last, "V")
        SS["pmul_V"][qi] = mk.cnt["V"]

    def _emit_tail(self, block, b, p, sb, A, SS):
        mk = self
        q1 = 128 * (2 * sb + 1)
        kcN = (q1 + 128) // 128
        jA0b = max(0, ((897 - q1) // 128) * 128) if q1 < 897 else 0
        jcs = list(range(jA0b // 128, 9))
        E1P, A2b, PTb, A2Tb = A["E1P"], A["A2b"], A["PTb"], A["A2Tb"]
        E2sb = A["E2sb"]
        Vtm, aob, rcbc, rcrow = A["Vtm"], A["aob"], A["rcbc"], A["rcrow"]
        pm_bf, pm2_bf = A["pm_bf"], A["pm2_bf"]
        FBH, FBP = A["FBH"], A["FBP"]

        prevEv = {}
        for s01 in range(2):
            qi = 2 * sb + s01
            q0 = 128 * qi
            kfar = q0 - 1024 if q0 >= 1024 else 0
            NZW = q0 + 128 - kfar
            jA0 = max(0, ((897 - q0) // 128) * 128) if q0 < 897 else 0

            @block.tensor
            def _(tensor, s01=s01, qi=qi, kfar=kfar, NZW=NZW):
                mk.wait(tensor, "V", SS["pmul_V"][qi])
                mk.wait(tensor, "S", SS["last_scoreS"][2 * sb + 1])
                if s01 == 1:
                    mk.wait(tensor, "V", prevEv[0])
                last = None
                for h in range(2):
                    src = E1P[2 * s01 + h]
                    for c in range(0, NZW, 512):
                        w = min(512, NZW - c)
                        dst = (self.pq2[h][:, c:c + w] if c < 1024 else
                               [self.pmini, self.pmini2][h][:, 0:w])
                        last = tensor.matmul(
                            dst, mk.antib[:],
                            src[:, 256 + kfar + c:256 + kfar + c + w],
                            start=True, stop=True)
                mk.inc(last, "T")
            flip_T = mk.cnt["T"]

            @block.vector
            def _(vector, s01=s01, NZW=NZW, flip_T=flip_T):
                mk.wait(vector, "T", flip_T)
                mk.wait(vector, "V", SS["pmul_V"][2 * sb + 1])
                if mk.cnt["qA"] > 0:
                    mk.wait(vector, "qA")
                last = None
                for h in range(2):
                    w0 = min(NZW, 1024)
                    last = vector.tensor_copy(
                        E2sb[h][:, 128:128 + w0], self.pq2[h][:, 0:w0])
                    if NZW > 1024:
                        last = vector.tensor_copy(
                            E2sb[h][:, 1152:128 + NZW],
                            [self.pmini, self.pmini2][h][:, 0:NZW - 1024])
                prevEv[s01] = mk.inc(last, "V")
                if s01 == 1:
                    SS["flip_ev"] = prevEv[1]

            @block.scalar
            def _(scalar, s01=s01, q0=q0, kfar=kfar, jA0=jA0):
                mk.wait(scalar, "V", prevEv[s01])
                if mk.cnt["qA"] > 0:
                    mk.wait(scalar, "qA")
                for h in range(2):
                    srcoff = (A["ESO"] + h * 1280 + 128 + jA0 + q0 + 127
                              - 1024 - kfar)
                    ap = bass.AP(FBH, srcoff,
                                 [[FBP - 1, 128], [1, 1025 - jA0]])
                    mk.inc(scalar.dma_start(
                        A2b[2 * s01 + h][:, jA0:1025], ap), "qA", 16)
        shearA_q = mk.cnt["qA"]

        jA0e = (max(0, ((897 - 256 * sb) // 128) * 128)
                if 256 * sb < 897 else 0)

        @block.vector
        def _(vector):
            if jA0b < jA0e:
                mk.wait(vector, "V", SS["tail_V"])
                for h in range(2):
                    vector.memset(A2b[h][:, jA0b:jA0e], 0.0)
            mk.inc(vector.engine_nop(), "V")
        gapz_V = mk.cnt["V"]

        avps = [self.pp[0][0:65, 0:256], self.pp[1][0:65, 0:256]]
        evac_V = {}
        avT = {}
        for kc in range(kcN):
            @block.tensor
            def _(tensor, kc=kc):
                if kc == 0:
                    mk.wait(tensor, "V", SS["pmul_V"][2 * sb + 1])
                    mk.wait(tensor, "V", SS["tail_V"])
                    mk.wait(tensor, "V", prevEv[1])
                if kc >= 1:
                    mk.wait(tensor, "V", evac_V[kc - 1])
                last = None
                for h in range(2):
                    pdst = [pm_bf, pm2_bf][h]
                    for s01 in range(2):
                        last = tensor.transpose(
                            pdst[:, 128 * s01:128 * s01 + 128],
                            E1P[2 * s01 + h][:, 256 + 128 * kc:
                                             256 + 128 * kc + 128],
                            mk.idenb[:])
                mk.inc(last, "T")
            tr_T = mk.cnt["T"]

            @block.vector
            def _(vector, kc=kc, tr_T=tr_T):
                mk.wait(vector, "T", tr_T)
                vector.tensor_copy(PTb[0][:], pm_bf[:, 0:256])
                evac_V[kc] = mk.inc(vector.tensor_copy(PTb[1][:],
                                                       pm2_bf[:, 0:256]),
                                    "V")

            @block.tensor
            def _(tensor, kc=kc):
                mk.wait(tensor, "V", evac_V[kc])
                last = None
                for h in range(2):
                    last = tensor.matmul(
                        avps[h], Vtm[:, 132 * kc + 66 * h:
                                     132 * kc + 66 * h + 65],
                        PTb[h][:], start=(kc == 0), stop=False,
                        skip_group_check=True)
                avT[kc] = mk.inc(last, "T")

        a2e_V = {}
        for ji, jc in enumerate(jcs):
            @block.tensor
            def _(tensor, ji=ji, jc=jc):
                if ji == 0:
                    mk.wait(tensor, "qA", shearA_q)
                    mk.wait(tensor, "V", evac_V[kcN - 1])
                if ji >= 1:
                    mk.wait(tensor, "V", a2e_V[ji - 1])
                last = None
                for h in range(2):
                    pdst = [pm_bf, pm2_bf][h]
                    for s01 in range(2):
                        last = tensor.transpose(
                            pdst[:, 128 * s01:128 * s01 + 128],
                            A2b[2 * s01 + h][:, 128 * jc:128 * jc + 128],
                            mk.antib[:])
                mk.inc(last, "T")
            a2t_T = mk.cnt["T"]

            @block.vector
            def _(vector, ji=ji, a2t_T=a2t_T):
                mk.wait(vector, "T", a2t_T)
                vector.tensor_copy(A2Tb[0][:], pm_bf[:, 0:256])
                a2e_V[ji] = mk.inc(vector.tensor_copy(A2Tb[1][:],
                                                      pm2_bf[:, 0:256]),
                                   "V")

            @block.tensor
            def _(tensor, ji=ji, jc=jc):
                mk.wait(tensor, "V", a2e_V[ji])
                last = None
                for h in range(2):
                    last = tensor.matmul(
                        avps[h][0:64, :], mk.pevs[:, 64 * jc:64 * jc + 64],
                        A2Tb[h][:], start=False, stop=(ji == len(jcs) - 1),
                        skip_group_check=True)
                mk.inc(last, "T")
        pev_T = mk.cnt["T"]

        @block.vector
        def _(vector):
            mk.wait(vector, "T", pev_T)
            vector.reciprocal(rcrow[0:1, 0:256], self.pp[0][64:65, 0:256])
            mk.inc(vector.reciprocal(rcrow[0:1, 256:512],
                                     self.pp[1][64:65, 0:256]), "V")
        rc_V = mk.cnt["V"]

        @block.tensor
        def _(tensor, rc_V=rc_V):
            mk.wait(tensor, "V", rc_V)
            tensor.matmul(self.pmini[:, 256:512], mk.onesrow[0:1, :],
                          rcrow[0:1, 0:256], start=True, stop=True)
            mk.inc(tensor.matmul(
                self.pmini2[:, 256:512], mk.onesrow[0:1, :],
                rcrow[0:1, 256:512], start=True, stop=True), "T")
        rcb_T = mk.cnt["T"]

        @block.vector
        def _(vector, sb=sb, rcb_T=rcb_T):
            mk.wait(vector, "T", rcb_T)
            v = mk.inc(vector.tensor_copy(rcbc[:, 0:256],
                                          self.pmini[:, 256:512]), "V")
            mk.inc(vector.tensor_copy(rcbc[:, 256:512],
                                      self.pmini2[:, 256:512]), "V")
            vector.wait_ge(mk.sem["V"], mk.cnt["V"])
            ao = 2048 * p + 256 * sb
            vector.tensor_mul(aob[0:64, ao:ao + 256],
                              self.pp[0][0:64, 0:256], rcbc[0:64, 0:256])
            mk.inc(vector.tensor_mul(
                aob[64:128, ao:ao + 256], self.pp[1][0:64, 0:256],
                rcbc[64:128, 256:512]), "V")
        SS["tail_V"] = mk.cnt["V"]

        if getattr(self, "debug_dump", False) and b == 0 and p == 0 \
                and sb in (0, 7):
            E1Pb, E2bb, A2bb = A["E1P"], A["E2b"], A["A2b"]
            Qfm, Kfm, Vtm = A["Qfm"], A["Kfm"], A["Vtm"]
            tv = SS["tail_V"]
            base = 0 if sb == 0 else 8448
            ao_d = 2048 * p + 256 * sb

            @block.vector
            def _(vector, tv=tv):
                mk.wait(vector, "V", tv)
                mk.inc(vector.engine_nop(), "V")
            dv = mk.cnt["V"]

            @block.gpsimd
            def _(g, dv=dv, base=base):
                mk.wait(g, "V", dv)
                if mk.cnt["qG"] > 0:
                    mk.wait(g, "qG")
                mk.inc(g.dma_start(self.dbg[:, base + 0:base + 2304],
                                   E1Pb[1][:, :]), "qG", 16)
                mk.inc(g.dma_start(self.dbg[:, base + 2304:base + 3584],
                                   E2bb[1][:, :]), "qG", 16)
                mk.inc(g.dma_start(self.dbg[:, base + 3584:base + 4736],
                                   A2bb[1][:, :]), "qG", 16)
                mk.inc(g.dma_start(self.dbg[:, base + 4736:base + 6784],
                                   Qfm[:, :]), "qG", 16)
                mk.inc(g.dma_start(self.dbg[:, base + 6784:base + 7040],
                                   A["E2sb"][1][:, 0:256]), "qG", 16)
                mk.inc(g.dma_start(self.dbg[:, base + 7040:base + 7304],
                                   Vtm[:, 0:264]), "qG", 16)
                mk.inc(g.dma_start(self.dbg[:, base + 7304:base + 7560],
                                   A["aob"][:, ao_d:ao_d + 256]), "qG", 16)
                mk.inc(g.dma_start(self.dbg[:, base + 7560:base + 7816],
                                   A["PTb"][1][:, :]), "qG", 16)
                mk.inc(g.dma_start(self.dbg[0:1, base + 7816:base + 8328],
                                   A["rcrow"][0:1, 0:512]), "qG", 16)
                mk.wait(g, "qG")
                mk.inc(g.engine_nop(), "G")
            dg = mk.cnt["G"]

            @block.vector
            def _(vector, dg=dg):
                mk.wait(vector, "G", dg)
                mk.inc(vector.engine_nop(), "V")
            SS["tail_V"] = mk.cnt["V"]

    def _emit_wo(self, block, b, A):
        mk = self
        aob = A["aob"]
        aF = self.arenaF
        aFB = self.aFB
        WSO = A["WSO"]

        xc = [aF[:, 0:4096], aF[:, 4096:8192]]
        xdb = [self.xd[128 * f:128 * f + 128, S * b:S * b + S]
               for f in range(NF)]

        wo_q = {}

        @block.gpsimd
        def _(g):
            s, v = A["prev_tail"]
            mk.wait(g, s, v)
            for half in range(2):
                if mk.cnt["qG"] > 0:
                    mk.wait(g, "qG")
                last = None
                for k in range(NF):
                    for oc in range(4):
                        c = WSO + 4096 * half + (k * 4 + oc) * 128
                        last = mk.inc(g.dma_start(
                            aFB[:, c:c + 128],
                            self.woT[128 * k:128 * k + 128,
                                     128 * (4 * half + oc):
                                     128 * (4 * half + oc) + 128]),
                            "qG", 16)
                wo_q[half] = last

        ldx = {}

        @block.sync
        def _(sync):
            s, v = A["prev_tail"]
            mk.wait(sync, s, v)
            for ncx in range(2):
                if mk.cnt["qL0"] > 0:
                    mk.wait(sync, "qL0")
                last = None
                for f in range(NF):
                    last = mk.inc(sync.dma_start(
                        xc[ncx][:, 512 * f:512 * f + 512],
                        xdb[f][:, 512 * ncx:512 * ncx + 512]), "qL0", 16)
                ldx[ncx] = last

        add_V = {}
        for ncx in range(4):
            bx = ncx % 2
            for oc2 in range(NF // 2):
                @block.tensor
                def _(tensor, ncx=ncx, oc2=oc2):
                    if ncx == 0 and oc2 == 0:
                        s, v = A["prev_tail"]
                        mk.wait(tensor, s, v)
                        mk.wait(tensor, "qG", wo_q[0])
                    if ncx == 0 and oc2 == 2:
                        mk.wait(tensor, "qG", wo_q[1])
                    for oc in (2 * oc2, 2 * oc2 + 1):
                        gi = ncx * NF + oc
                        if gi - 2 >= 0 and (gi - 2) in add_V:
                            mk.wait(tensor, "V", add_V[gi - 2])
                        half, oc4 = divmod(oc, 4)
                        po = self.pq2[oc % 2][:, 0:512]
                        last = None
                        for k in range(NF):
                            c = WSO + 4096 * half + (k * 4 + oc4) * 128
                            last = tensor.matmul(
                                po, aFB[:, c:c + 128],
                                aob[:, 2048 * k + 512 * ncx:
                                    2048 * k + 512 * ncx + 512],
                                start=(k == 0), stop=(k == NF - 1))
                        mk.inc(last, "T")
                woT2 = mk.cnt["T"]

                @block.vector
                def _(vector, ncx=ncx, bx=bx, oc2=oc2, woT2=woT2):
                    mk.wait(vector, "T", woT2)
                    mk.wait(vector, "qL0", ldx[ncx])
                    for oc in (2 * oc2, 2 * oc2 + 1):
                        gi = ncx * NF + oc
                        v = mk.inc(vector.tensor_add(
                            xc[bx][:, 512 * oc:512 * oc + 512],
                            xc[bx][:, 512 * oc:512 * oc + 512],
                            self.pq2[oc % 2][:, 0:512]), "V")
                        add_V[gi] = v
            chunk_add = mk.cnt["V"]

            @block.sync
            def _(sync, ncx=ncx, bx=bx, chunk_add=chunk_add):
                mk.wait(sync, "V", chunk_add)
                if mk.cnt["qSt"] > 0:
                    mk.wait(sync, "qSt")
                for f in range(NF):
                    mk.inc(sync.dma_start(
                        xdb[f][:, 512 * ncx:512 * ncx + 512],
                        xc[bx][:, 512 * f:512 * f + 512]), "qSt", 16)
                if ncx + 2 < 4:
                    mk.wait(sync, "qSt")
                    mk.wait(sync, "qL0")
                    last = None
                    for f in range(NF):
                        last = mk.inc(sync.dma_start(
                            xc[bx][:, 512 * f:512 * f + 512],
                            xdb[f][:, 512 * (ncx + 2):
                                   512 * (ncx + 2) + 512]), "qL0", 16)
                    ldx[ncx + 2] = last

    def p_finish(self, block):
        mk = self

        @block.sync
        def _(sync):
            for q in ("qS", "qG", "qA", "qL0", "qL1", "qSt"):
                if mk.cnt[q] > 0:
                    mk.wait(sync, q)


def build_program(stop_after="full"):
    nc = bass.Bass("TRN2", target_bir_lowering=False)
    with ExitStack() as st:
        mk = MK(nc, st, stop_after)
        block = st.enter_context(nc.Block())
        mk.p_init(block)
        if stop_after == "ffn1":
            mk.p_ffn(block, 0, mk.xin, mk.out)
        else:
            w = mk.p_ffn(block, 0, mk.xin, mk.xd)
            if stop_after in ("full", "attn"):
                w = mk.p_attn(block, 0, pre_wait=w)
                w = mk.p_attn(block, 1, pre_wait=w)
            if stop_after == "attn":
                @block.sync
                def _(sync):
                    for s, v in w:
                        mk.wait(sync, s, v)
                    mk.inc(sync.dma_start(mk.out[:, :], mk.xd[:, :]),
                           "qSt", 16)
                    mk.wait(sync, "qSt")
            else:
                mk.p_ffn(block, 1, mk.xd, mk.out, pre_wait=w)
        mk.p_finish(block)
    return nc


def make_in_maps(L0_attn_x, L0_attn_Wq, L0_attn_Wk, L0_attn_Wv, L0_attn_Wo,
                 L0_attn_pek, L0_attn_pev, L0_ffn_w_gate, L0_ffn_w_expert,
                 L0_ffn_b_expert):
    x = np.asarray(L0_attn_x, dtype=np.float32)
    pek = np.asarray(L0_attn_pek, np.float32)
    pev = np.asarray(L0_attn_pev, np.float32)
    pekT = np.zeros((128, JPAD), np.float32)
    pekT[0:64, 0:JW] = pek[0:JW].T
    pekT[64:128, 0:JW] = pek[0:JW].T
    pevp = np.zeros((JPAD, HD), np.float32)
    pevp[0:JW] = pev[0:JW]

    common = {
        "wqT": np.ascontiguousarray(np.asarray(L0_attn_Wq, np.float32).T),
        "wkT": np.ascontiguousarray(np.asarray(L0_attn_Wk, np.float32).T),
        "wvT": np.ascontiguousarray(np.asarray(L0_attn_Wv, np.float32).T),
        "woT": np.ascontiguousarray(np.asarray(L0_attn_Wo, np.float32).T),
        "wed": np.ascontiguousarray(np.asarray(L0_ffn_w_expert, np.float32)),
        "wgd": np.ascontiguousarray(np.asarray(L0_ffn_w_gate, np.float32)),
        "pekTd": pekT, "pevd": pevp,
        "bexpd": np.ascontiguousarray(
            np.asarray(L0_ffn_b_expert, np.float32).reshape(1, D)),
        "idend": np.eye(128, dtype=np.float32),
        "antid": np.ascontiguousarray(np.eye(128, dtype=np.float32)[::-1]),
        "iota8d": np.arange(E, dtype=np.float32).reshape(E, 1),
    }
    in_maps = []
    for c in range(NCORES):
        xb = x[2 * c:2 * c + 2].reshape(T, D)
        sel = np.zeros((E, E), np.float32)
        sel[:, 0:c] = 1.0
        m = dict(common)
        m["xin"] = np.ascontiguousarray(xb.T)
        m["selmatd"] = sel
        in_maps.append(m)
    return in_maps


def kernel(**inputs):
    in_maps = make_in_maps(**inputs)
    nc = build_program("full")
    res = run_bass_kernel_spmd(nc, in_maps, core_ids=list(range(NCORES)))
    outs = []
    for c in range(NCORES):
        ofm = np.asarray(res.results[c]["out"])
        outs.append(ofm.T.reshape(B, S, D))
    return np.concatenate(outs, axis=0)
